# version 11
# speedup vs baseline: 1.4398x; 1.0817x over previous
import math
from contextlib import ExitStack

import numpy as np
import ml_dtypes
import concourse.bass as bass
import concourse.mybir as mybir
from concourse.bass_utils import run_bass_kernel_spmd

F32 = mybir.dt.float32
BF16 = mybir.dt.bfloat16
AF = mybir.ActivationFunctionType
ALU = mybir.AluOpType
AX = mybir.AxisListType
BF16NP = ml_dtypes.bfloat16

B, S, D = 2, 4096, 2048
T = B * S
NT = T // 8
NH, DH = 16, 128
HPC = 4
INTER0 = 8192
ISL0 = INTER0 // 8
NE, IE = 16, 1408
ISH = 2816
EPS = 1e-5
NDC = D // 128

ENGINES = ("tensor", "vector", "scalar", "sync", "gpsimd")


_SEM_UID = [0]


class Sem:
    def __init__(self, ctx, nc, name):
        _SEM_UID[0] += 1
        self.h = ctx.enter_context(nc.semaphore(f"{name}_{_SEM_UID[0]}"))
        self.v = 0

    def post(self, n=1):
        self.v += n
        return self.v


class Prog:

    def __init__(self, nc, ctx):
        self.nc = nc
        self.ctx = ctx
        self.steps = {e: [] for e in ENGINES}

    def sem(self, name):
        sm = Sem(self.ctx, self.nc, name)
        if not hasattr(self, "all_sems"):
            self.all_sems = []
        self.all_sems.append(sm)
        return sm

    def op(self, eng, fn, sem=None, inc=1):
        if not hasattr(self, "_last"):
            self._last = {}
        if sem is not None:
            sem.post(inc)

            def run(e, f=fn, s=sem.h, n=inc, eng=eng):
                ins = f(e).then_inc(s, n)
                self._last[eng] = ins
                if eng == "tensor":
                    self._marked = id(ins)
                return ins
        else:
            def run(e, f=fn, eng=eng):
                ins = f(e)
                self._last[eng] = ins
                return ins
        self.steps[eng].append(run)

    def wait(self, eng, sem, v=None):
        vv = sem.v if v is None else v
        self.steps[eng].append(lambda e, s=sem.h, w=vv: e.wait_ge(s, w))

    def make_consts(self, ctx, nc, s_init):
        self._cval = {}
        self._cbuf = _sb(ctx, nc, "cconsts", [1, 32])
        self._cbuf128 = _sb(ctx, nc, "cconsts128", [128, 8])
        self._cn = 0
        self._cn128 = 0
        self._cs = s_init

    def cell(self, value):
        key = (1, float(value))
        if key not in self._cval:
            i = self._cn
            self._cn += 1
            self.op("vector", lambda v, i=i, val=float(value):
                    v.memset(self._cbuf[0:1, i:i + 1], val), sem=self._cs)
            self._cval[key] = self._cbuf[0:1, i:i + 1]
        return self._cval[key]

    def cell128(self, value):
        key = (128, float(value))
        if key not in self._cval:
            i = self._cn128
            self._cn128 += 1
            self.op("vector", lambda v, i=i, val=float(value):
                    v.memset(self._cbuf128[:, i:i + 1], val), sem=self._cs)
            self._cval[key] = self._cbuf128[:, i:i + 1]
        return self._cval[key]

    def warmup_scalar(self, ctx, nc, funcs, sem):
        wb = _sb(ctx, nc, "warmb", [1, 16])
        wc = _sb(ctx, nc, "warmc", [1, 16])
        self.op("vector", lambda v: v.memset(wb[0:1, :], 0.25), sem=sem)
        self.wait("scalar", sem)
        for f in funcs:
            for _ in range(2):
                self.op("scalar", lambda s, f=f: s.activation(wc[0:1, :], wb[0:1, :], f),
                        sem=sem)
        self.wait("scalar", sem)

    TNOP_DRAIN = True
    _marked = None

    def tnop(self, sem):
        if self.TNOP_DRAIN:
            self.op("tensor", lambda t: t.drain(), sem=sem)
        else:
            sem.post(1)

            def run(e, s=sem.h):
                last = self._last.get("tensor")
                if last is not None and id(last) != self._marked:
                    self._marked = id(last)
                    last.then_inc(s, 1)
                else:
                    ins = e.drain().then_inc(s, 1)
                    self._last["tensor"] = ins
                    self._marked = id(ins)
            self.steps["tensor"].append(run)

    def dma(self, q, out, in_, sem=None, slow=False):
        self.op(q, lambda e, o=out, i=in_, sl=slow: e.dma_start(
            out=o, in_=i, allow_slow_non_contiguous=sl), sem, 16)
        if sem is not None:
            self.wait(q, sem)

    def emit(self):
        sems = getattr(self, "all_sems", [])
        ncl = getattr(self, "_ncleared", 0)
        new = sems[ncl:]
        self._ncleared = len(sems)
        if new:
            with self.nc.Block() as bl0:
                def _clr(g, hs=[sm.h for sm in new]):
                    for h in hs:
                        g.sem_clear(h)
                bl0.gpsimd(_clr)
        with self.nc.Block() as bl:
            for name in ENGINES:
                mysteps = self.steps[name]
                if not mysteps:
                    continue
                reg = getattr(bl, name)

                def closure(eng, fns=tuple(mysteps)):
                    for f in fns:
                        f(eng)
                reg(closure)
        self.steps = {e: [] for e in ENGINES}


def _ceil(a, b):
    return -(-a // b)


def _sb(ctx, nc, name, shape, dtype=F32):
    return ctx.enter_context(nc.sbuf_tensor(name, shape, dtype,
                                            allow_name_mangling=True))


def _ps(ctx, nc, name, shape, dtype=F32):
    return ctx.enter_context(nc.psum_tensor(name, shape, dtype,
                                            allow_name_mangling=True))


def strips(nt, w=512):
    return [(s * w, min(nt, (s + 1) * w)) for s in range(_ceil(nt, w))]


def tok_ap_multi(src, t0, w, nchunk=16, i0=0):
    ten, kind = src[0], src[1]
    if kind == "flat":
        Ttot = ten.shape[1]
        return bass.AP(ten, (i0 * 128) * Ttot + t0,
                       [[Ttot, 128], [128 * Ttot, nchunk], [1, w]])
    g = src[2] + t0
    r = g // NT
    tl = g - r * NT
    assert tl + w <= NT, (t0, w)
    return bass.AP(ten, (r * D + i0 * 128) * NT + tl,
                   [[NT, 128], [128 * NT, nchunk], [1, w]])


def tok_ap(src, i, t0, w):
    ten, kind = src[0], src[1]
    if kind == "flat":
        return ten[i * 128:(i + 1) * 128, t0:t0 + w]
    g = src[2] + t0
    r = g // NT
    tl = g - r * NT
    assert tl + w <= NT, (t0, w)
    return ten[r * D + i * 128: r * D + (i + 1) * 128, tl:tl + w]






def emit_ln(p, ctx, nc, tag, x_tiles, out_tiles, nt, ready, gain=None, bias=None):
    ones_r = p.ones_r
    ones_b = p.ones_b
    sqf = _sb(ctx, nc, tag + "_sqf", [128, 16 * nt])
    pstat = _ps(ctx, nc, tag + "_ps", [128, nt])
    pstatq = _ps(ctx, nc, tag + "_psq", [128, nt])
    mean = _sb(ctx, nc, tag + "_mean", [1, nt])
    m2 = _sb(ctx, nc, tag + "_m2", [1, nt])
    varb = _sb(ctx, nc, tag + "_varb", [1, nt])
    var2 = _sb(ctx, nc, tag + "_var2", [1, nt])
    lnb = _sb(ctx, nc, tag + "_lnb", [1, nt])
    rstd = _sb(ctx, nc, tag + "_rstd", [1, nt])
    pb1 = _ps(ctx, nc, tag + "_pb1", [128, nt])
    pb2 = _ps(ctx, nc, tag + "_pb2", [128, nt])
    dbuf = [_sb(ctx, nc, tag + f"_d{k}", [128, nt]) for k in range(2)]
    s_sq = p.sem(tag + "sq")
    s_st = p.sem(tag + "st")
    s_gl = p.sem(tag + "gl")
    s_rs = p.sem(tag + "rs")
    s_bc = p.sem(tag + "bc")
    s_ap = p.sem(tag + "ap")

    for eng, sem in ready:
        p.wait(eng, sem)

    for (c0, c1) in strips(nt):
        for i in range(16):
            p.op("tensor", lambda t, i=i, c0=c0, c1=c1:
                 t.matmul(pstat[0:1, c0:c1], ones_r[:, 0:1], x_tiles[i][:, c0:c1],
                          start=(i == 0), stop=(i == 15)))
    for i in range(16):
        p.op("scalar", lambda s, i=i:
             s.activation(sqf[:, i * nt:(i + 1) * nt], x_tiles[i], AF.Square),
             sem=s_sq)
    p.wait("tensor", s_sq)
    for (c0, c1) in strips(nt):
        for i in range(16):
            p.op("tensor", lambda t, i=i, c0=c0, c1=c1:
                 t.matmul(pstatq[0:1, c0:c1], ones_r[:, 0:1],
                          sqf[:, i * nt + c0: i * nt + c1],
                          start=(i == 0), stop=(i == 15)))
    p.tnop(s_st)
    p.wait("vector", s_st)
    p.op("vector", lambda v: v.tensor_scalar(mean[0:1, :], pstat[0:1, :],
                                             1.0 / D, None, ALU.mult), sem=s_gl)
    p.op("vector", lambda v: v.tensor_scalar(varb[0:1, :], pstatq[0:1, :],
                                             1.0 / D, None, ALU.mult), sem=s_gl)
    p.wait("vector", s_gl)
    p.op("vector", lambda v: v.tensor_tensor(m2[0:1, :], mean[0:1, :],
                                             mean[0:1, :], ALU.mult), sem=s_gl)
    p.wait("vector", s_gl)
    p.op("vector", lambda v: v.tensor_sub(var2[0:1, :], varb[0:1, :], m2[0:1, :]),
         sem=s_gl)
    p.wait("scalar", s_gl)
    epsc = p.cell(EPS)
    p.op("scalar", lambda s: s.activation(lnb[0:1, :], var2[0:1, :], AF.Ln,
                                          bias=epsc), sem=s_rs)
    p.wait("scalar", s_rs)
    p.op("scalar", lambda s: s.activation(rstd[0:1, :], lnb[0:1, :], AF.Exp,
                                          scale=-0.5), sem=s_rs)
    p.wait("tensor", s_gl)
    p.wait("tensor", s_rs)
    for (c0, c1) in strips(nt):
        p.op("tensor", lambda t, c0=c0, c1=c1:
             t.matmul(pb1[:, c0:c1], ones_b[0:1, :], mean[0:1, c0:c1],
                      start=True, stop=True))
        p.op("tensor", lambda t, c0=c0, c1=c1:
             t.matmul(pb2[:, c0:c1], ones_b[0:1, :], rstd[0:1, c0:c1],
                      start=True, stop=True))
    p.tnop(s_bc)
    p.wait("vector", s_bc)
    for i in range(16):
        p.op("vector", lambda v, i=i:
             v.tensor_sub(dbuf[i % 2][:, :], x_tiles[i], pb1[:, :]), sem=s_ap)
        p.wait("vector", s_ap)
        if gain is None:
            p.op("vector", lambda v, i=i:
                 v.tensor_mul(out_tiles[i], dbuf[i % 2][:, :], pb2[:, :]), sem=s_ap)
        else:
            p.op("vector", lambda v, i=i:
                 v.tensor_mul(dbuf[i % 2][:, :], dbuf[i % 2][:, :], pb2[:, :]),
                 sem=s_ap)
            p.wait("vector", s_ap)
            p.op("vector", lambda v, i=i:
                 v.tensor_scalar(out_tiles[i], dbuf[i % 2][:, :],
                                 gain[i], bias[i], ALU.mult, ALU.add), sem=s_ap)
    return s_ap



def build_launch_A(nc=None, io=None):
    own = nc is None
    if own:
        nc = bass.Bass("TRN2", target_bir_lowering=False)
    io = io or {}
    xT = io.get("xT") or nc.dram_tensor("xT", [D, NT], F32, kind="ExternalInput")
    xnT = io.get("xnT") or nc.dram_tensor("xnT", [D, NT], BF16, kind="ExternalOutput")
    ctx = ExitStack()
    with ctx:
        p = Prog(nc, ctx)
        p.ones_r = _sb(ctx, nc, "ones_r", [128, 1])
        p.ones_b = _sb(ctx, nc, "ones_b", [1, 128])
        xt = [_sb(ctx, nc, f"xt{i}", [128, NT]) for i in range(16)]
        xnb = [_sb(ctx, nc, f"xnb{i}", [128, NT], BF16) for i in range(16)]
        s_load = p.sem("load")
        s_out = p.sem("out")

        p.make_consts(ctx, nc, s_load)
        p.op("vector", lambda v: v.memset(p.ones_r[:, 0:1], 1.0), sem=s_load)
        p.op("vector", lambda v: v.memset(p.ones_b[0:1, :], 1.0), sem=s_load)
        for i in range(16):
            p.dma("sync", xt[i][:, :], xT[i * 128:(i + 1) * 128, :], s_load)
        ready = [(e, s_load) for e in ("tensor", "scalar", "vector")]
        s_ap = emit_ln(p, ctx, nc, "ln", [t[:, :] for t in xt],
                       [t[:, :] for t in xnb], NT, ready)
        p.wait("sync", s_ap)
        for i in range(16):
            p.dma("sync", xnT[i * 128:(i + 1) * 128, :], xnb[i][:, :], s_out)
        p.wait("sync", s_out)
        p.emit()
    return nc



def emit_spline(p, tag, pk_dram, ck_dram, out_tiles, nparts, nfree, s_sp, sh,
                csplit=1):
    palls, calls, acc, tmp, s_ld = sh["pb"], sh["cb"], sh["acc"], sh["tmp"], sh["s_ld"]
    nf2 = nfree // csplit
    kg = sh.setdefault("kg", [0])
    for i in range(nparts):
        r0 = i * 128
        for ch in range(csplit):
            par = kg[0] % 2
            pall, call_ = palls[par], calls[par]
            pb = [pall[:, k * nf2:(k + 1) * nf2] for k in range(4)]
            cb = [call_[:, k * nf2:(k + 1) * nf2] for k in range(4)]
            if kg[0] >= 2:
                p.wait("sync", s_sp, 7 * (kg[0] - 1))
                p.wait("scalar", s_sp, 7 * (kg[0] - 1))
            kg[0] += 1
            src_p = bass.AP(pk_dram, r0 * 4 * nfree + ch * nf2,
                            [[4 * nfree, 128], [nfree, 4], [1, nf2]])
            src_c = bass.AP(ck_dram, r0 * 4 * nfree + ch * nf2,
                            [[4 * nfree, 128], [nfree, 4], [1, nf2]])
            p.dma("sync", pall[:, 0:4 * nf2], src_p, s_ld)
            p.dma("scalar", call_[:, 0:4 * nf2], src_c, s_ld)
            p.wait("vector", s_ld)
            p.op("vector", lambda v, acc=acc, cb=cb, pb=pb, nf2=nf2:
                 v.tensor_mul(acc[:, 0:nf2], cb[0], pb[0]), sem=s_sp)
            p.op("vector", lambda v, tmp=tmp, cb=cb, pb=pb, nf2=nf2:
                 v.tensor_mul(tmp[:, 0:nf2], cb[1], pb[1]), sem=s_sp)
            p.wait("vector", s_sp)
            p.op("vector", lambda v, acc=acc, tmp=tmp, nf2=nf2:
                 v.tensor_add(acc[:, 0:nf2], acc[:, 0:nf2], tmp[:, 0:nf2]), sem=s_sp)
            p.op("vector", lambda v, tmp=tmp, cb=cb, pb=pb, nf2=nf2:
                 v.tensor_mul(tmp[:, 0:nf2], cb[2], pb[2]), sem=s_sp)
            p.wait("vector", s_sp)
            p.op("vector", lambda v, acc=acc, tmp=tmp, nf2=nf2:
                 v.tensor_add(acc[:, 0:nf2], acc[:, 0:nf2], tmp[:, 0:nf2]), sem=s_sp)
            p.op("vector", lambda v, tmp=tmp, cb=cb, pb=pb, nf2=nf2:
                 v.tensor_mul(tmp[:, 0:nf2], cb[3], pb[3]), sem=s_sp)
            p.wait("vector", s_sp)
            p.op("vector", lambda v, i=i, ch=ch, tmp=tmp, acc=acc, nf2=nf2:
                 v.scalar_tensor_tensor(
                     out_tiles[i][:, ch * nf2:(ch + 1) * nf2], tmp[:, 0:nf2], 1.0,
                     acc[:, 0:nf2], ALU.mult, ALU.add), sem=s_sp)


def build_launch_C(nc=None, io=None):
    own = nc is None
    if own:
        nc = bass.Bass("TRN2", target_bir_lowering=False)
    io = io or {}
    xnF = io.get("xnF")
    if xnF is None:
        xnF = (nc.dram_tensor("xnF", [D, T], BF16, kind="ExternalInput"), "flat")
    gp = nc.dram_tensor("g_p", [D, 4 * ISL0], F32, kind="ExternalInput")
    gc = nc.dram_tensor("g_c", [D, 4 * ISL0], F32, kind="ExternalInput")
    up = nc.dram_tensor("u_p", [D, 4 * ISL0], F32, kind="ExternalInput")
    uc = nc.dram_tensor("u_c", [D, 4 * ISL0], F32, kind="ExternalInput")
    dp = nc.dram_tensor("d_p", [ISL0, 4 * D], F32, kind="ExternalInput")
    dc = nc.dram_tensor("d_c", [ISL0, 4 * D], F32, kind="ExternalInput")
    out = io.get("ffn0sum") or nc.dram_tensor("ffn0sum", [D, NT], BF16, kind="ExternalOutput")
    rs_in = nc.dram_tensor("rs_inC", [8 * D, NT], BF16)
    rs_out = nc.dram_tensor("rs_outC", [D, NT], BF16)

    ctx = ExitStack()
    with ctx:
        p = Prog(nc, ctx)
        s_init = p.sem("init")
        p.make_consts(ctx, nc, s_init)
        gt = [_sb(ctx, nc, f"gt{i}", [128, ISL0], BF16) for i in range(16)]
        ut = [_sb(ctx, nc, f"ut{i}", [128, ISL0], BF16) for i in range(16)]
        dt = [_sb(ctx, nc, f"dt{i}", [128, D], BF16) for i in range(8)]
        s_sp = p.sem("spl")
        sctx = ExitStack()
        with sctx:
            sh = {
                "pb": [_sb(sctx, nc, f"sh_pall{k}", [128, 4 * 1024]) for k in range(2)],
                "cb": [_sb(sctx, nc, f"sh_call{k}", [128, 4 * 1024]) for k in range(2)],
                "acc": _sb(sctx, nc, "sh_acc", [128, 1024]),
                "tmp": _sb(sctx, nc, "sh_tmp", [128, 1024]),
                "s_ld": p.sem("shld"),
            }
            emit_spline(p, "sg", gp, gc, [a[:, :] for a in gt], 16, ISL0, s_sp, sh)
            emit_spline(p, "su", up, uc, [a[:, :] for a in ut], 16, ISL0, s_sp, sh)
            emit_spline(p, "sd", dp, dc, [a[:, :] for a in dt], 8, D, s_sp, sh, csplit=2)
        spl_done = s_sp.v

        xnt = _sb(ctx, nc, "xntC", [128, 16 * NT], BF16)
        act = [_sb(ctx, nc, f"act{i}", [128, NT], BF16) for i in range(8)]
        silg = [_sb(ctx, nc, f"silg{k}", [128, 512], BF16) for k in range(2)]
        ob = [_sb(ctx, nc, f"ob{k}", [128, 512], BF16) for k in range(2)]
        psG = [_ps(ctx, nc, f"psG{k}", [128, 512]) for k in range(2)]
        psU = [_ps(ctx, nc, f"psU{k}", [128, 512]) for k in range(2)]
        psD = [_ps(ctx, nc, f"psD{k}", [128, 512]) for k in range(2)]
        s_x = p.sem("x")
        s_g1 = p.sem("g1")
        s_g2 = p.sem("g2")
        s_g3 = p.sem("g3")
        s_d1 = p.sem("d1")
        s_d2 = p.sem("d2")
        s_cc = p.sem("cc")
        s_st = p.sem("st")

        p.wait("tensor", s_sp, spl_done)
        k1 = 0
        k2 = 0
        for r in range(8):
            t0 = r * NT
            if r == 0:
                p.wait("sync", s_sp, spl_done)
            p.dma("sync", xnt[:, :], tok_ap_multi(xnF, t0, NT), s_x)
            p.wait("tensor", s_x)
            for ic in range(8):
                for (c0, c1) in strips(NT):
                    par = k1 % 2
                    if k1 >= 2:
                        p.wait("tensor", s_g3, s_g3.v - 1)
                    for dcc in range(16):
                        p.op("tensor", lambda t, ic=ic, dcc=dcc, c0=c0, c1=c1, par=par:
                             t.matmul(psG[par][:, :],
                                      gt[dcc][:, ic * 128:(ic + 1) * 128],
                                      xnt[:, dcc * NT + c0: dcc * NT + c1],
                                      start=(dcc == 0), stop=(dcc == 15)))
                    for dcc in range(16):
                        p.op("tensor", lambda t, ic=ic, dcc=dcc, c0=c0, c1=c1, par=par:
                             t.matmul(psU[par][:, :],
                                      ut[dcc][:, ic * 128:(ic + 1) * 128],
                                      xnt[:, dcc * NT + c0: dcc * NT + c1],
                                      start=(dcc == 0), stop=(dcc == 15)))
                    p.tnop(s_g1)
                    p.wait("scalar", s_g1)
                    if k1 >= 2:
                        p.wait("scalar", s_g3, s_g3.v - 1)
                    p.op("scalar", lambda s, par=par:
                         s.activation(silg[par][:, :], psG[par][:, :], AF.Silu),
                         sem=s_g2)
                    p.wait("vector", s_g2)
                    p.op("vector", lambda v, ic=ic, c0=c0, c1=c1, par=par:
                         v.tensor_mul(act[ic][:, c0:c1], silg[par][:, :],
                                      psU[par][:, :]), sem=s_g3)
                    k1 += 1
            p.wait("tensor", s_g3)
            for dcc in range(16):
                for (c0, c1) in strips(NT):
                    par = k2 % 2
                    if k2 >= 2:
                        p.wait("tensor", s_d2, s_d2.v - 33)
                    for ic in range(8):
                        p.op("tensor", lambda t, dcc=dcc, ic=ic, c0=c0, c1=c1, par=par:
                             t.matmul(psD[par][:, :],
                                      dt[ic][:, dcc * 128:(dcc + 1) * 128],
                                      act[ic][:, c0:c1],
                                      start=(ic == 0), stop=(ic == 7)))
                    p.tnop(s_d1)
                    p.wait("vector", s_d1)
                    if k2 >= 2:
                        p.wait("vector", s_d2, s_d2.v - 17)
                    p.op("vector", lambda v, par=par:
                         v.tensor_copy(ob[par][:, :], psD[par][:, :]), sem=s_d2)
                    p.wait("sync", s_d2)
                    p.dma("sync", rs_in[r * D + dcc * 128: r * D + (dcc + 1) * 128,
                                        c0:c1], ob[par][:, :], s_d2)
                    k2 += 1
        p.wait("gpsimd", s_d2)
        p.op("gpsimd", lambda g: g.collective_compute(
            "ReduceScatter", ALU.add, replica_groups=[list(range(8))],
            ins=[rs_in.ap().opt()], outs=[rs_out.ap().opt()]), sem=s_cc)
        p.wait("gpsimd", s_cc)
        p.dma("gpsimd", out[:, :], rs_out[:, :], s_st)
        p.wait("gpsimd", s_st)
        p.emit()
    return nc



def build_launch_B(debug=False, nc=None, io=None):
    own = nc is None
    if own:
        nc = bass.Bass("TRN2", target_bir_lowering=False)
    io = io or {}
    xnB = io.get("xnB")
    if xnB is None:
        xnB = (nc.dram_tensor("xnB", [D, S], BF16, kind="ExternalInput"), "flat")
    WqT = nc.dram_tensor("WqT", [D, 512], F32, kind="ExternalInput")
    WkT = nc.dram_tensor("WkT", [D, 512], F32, kind="ExternalInput")
    WvT = nc.dram_tensor("WvT", [D, 512], F32, kind="ExternalInput")
    WoT = nc.dram_tensor("WoT", [512, D], F32, kind="ExternalInput")
    masksD = nc.dram_tensor("masks", [4 * 128, 512], BF16, kind="ExternalInput")
    out = io.get("y0sum") or nc.dram_tensor("y0sum", [D, NT], BF16, kind="ExternalOutput")
    if debug:
        dbg_r01 = nc.dram_tensor("dbg_r01", [128, 4 * 512], BF16, kind="ExternalOutput")
        dbg_wq = nc.dram_tensor("dbg_wq", [128, 16 * 512], BF16, kind="ExternalOutput")
        dbg_qe = nc.dram_tensor("dbg_qe", [512, S], BF16, kind="ExternalOutput")
        dbg_q = nc.dram_tensor("dbg_q", [512, S], BF16, kind="ExternalOutput")
        dbg_k = nc.dram_tensor("dbg_k", [512, S], BF16, kind="ExternalOutput")
        dbg_v = nc.dram_tensor("dbg_v", [128, 4 * S], BF16, kind="ExternalOutput")
        dbg_z = nc.dram_tensor("dbg_z", [512, S], BF16, kind="ExternalOutput")
    rs_in = nc.dram_tensor("rs_inB", [4 * D, NT], BF16)
    rs_out = nc.dram_tensor("rs_outB", [D, NT], BF16)
    SC = 1.0 / math.sqrt(DH)

    ctx = ExitStack()
    with ctx:
        p = Prog(nc, ctx)
        s_init = p.sem("init")
        p.make_consts(ctx, nc, s_init)
        ones128 = _sb(ctx, nc, "ones128", [128, 128], BF16)
        p.op("vector", lambda v: v.memset(ones128[:, :], 1.0), sem=s_init)
        qt = [_sb(ctx, nc, f"qt{h}", [128, S], BF16) for h in range(4)]
        kt = [_sb(ctx, nc, f"kt{h}", [128, S], BF16) for h in range(4)]
        vTP = _sb(ctx, nc, "vTP", [128, 4 * S], BF16)
        xnst = [_sb(ctx, nc, f"xnst{k}", [128, 16 * 512], BF16) for k in range(2)]
        masks = _sb(ctx, nc, "masks_sb", [128, 4 * 512], BF16)
        Eb = [_sb(ctx, nc, f"Eb{k}", [128, 512], BF16) for k in range(2)]
        rec = _sb(ctx, nc, "rec", [128, 512])
        ob = [_sb(ctx, nc, f"ob{k}", [128, 512], BF16) for k in range(2)]
        psP = [_ps(ctx, nc, f"psP{k}", [128, 512]) for k in range(2)]
        psZ = _ps(ctx, nc, "psZ", [128, 512])
        psDen = _ps(ctx, nc, "psDen", [128, 512])
        psQD = [_ps(ctx, nc, f"psQD{k}", [128, 512]) for k in range(2)] if debug else None
        r01 = _sb(ctx, nc, "r01", [128, 4 * 512], BF16) if debug else None

        s_ld = p.sem("ld")
        s_wu = p.sem("wu")
        p.warmup_scalar(ctx, nc, [AF.Copy, AF.Exp, AF.Silu], s_wu)
        s_wo = p.sem("swo")
        s_xn = p.sem("xn")
        s_pj = p.sem("pj")
        s_pe = p.sem("pe")
        s_a1 = p.sem("a1")
        s_aE = p.sem("aE")
        s_aM = p.sem("aM")
        s_a3 = p.sem("a3")
        s_zv = p.sem("zv")
        s_zn = p.sem("zn")
        s_y1 = p.sem("y1")
        s_y2 = p.sem("y2")
        s_cc = p.sem("cc")
        s_st = p.sem("st")

        for off in range(4):
            p.dma("sync", masks[:, off * 512:(off + 1) * 512],
                  masksD[off * 128:(off + 1) * 128, :], s_ld)

        wctx = ExitStack()
        with wctx:
            wq = _sb(wctx, nc, "wq", [128, 16 * 512], BF16)
            wk = _sb(wctx, nc, "wk", [128, 16 * 512], BF16)
            wv = _sb(wctx, nc, "wv", [128, 16 * 512], BF16)
            wstg = _sb(wctx, nc, "wstgB", [128, 8 * 512])
            s_wst = p.sem("wst")
            for (wsrc, wdst) in ((WqT, wq), (WkT, wk), (WvT, wv)):
                for half in range(2):
                    p.wait("sync", s_wst)
                    p.dma("sync", wstg[:, :],
                          bass.AP(wsrc, half * 8 * 128 * 512,
                                  [[512, 128], [128 * 512, 8], [1, 512]]), s_ld)
                    p.wait("vector", s_ld)
                    p.op("vector", lambda v, wdst=wdst, half=half:
                         v.tensor_copy(wdst[:, half * 4096:(half + 1) * 4096],
                                       wstg[:, :]), sem=s_wst)
            kk = 0
            for sidx in range(8):
                sp = sidx % 2
                c0 = sidx * 512
                if sidx >= 2:
                    p.wait("sync", s_xn, sidx - 1)
                p.dma("sync", xnst[sp][:, :], tok_ap_multi(xnB, c0, 512), s_ld)
                p.wait("tensor", s_ld)
                for (wbuf, dsts, scale) in ((wq, qt, SC), (wk, kt, 1.0)):
                    for h in range(4):
                        par = kk % 2
                        pdst = psQD[kk] if (debug and kk < 2) else psP[par]
                        if kk >= 2:
                            p.wait("tensor", s_pe, s_pe.v - 1)
                        for dcc in range(16):
                            p.op("tensor", lambda t, w=wbuf, h=h, dcc=dcc, sp=sp, pdst=pdst:
                                 t.matmul(pdst[:, :],
                                          w[:, dcc * 512 + h * 128: dcc * 512 + (h + 1) * 128],
                                          xnst[sp][:, dcc * 512:(dcc + 1) * 512],
                                          start=(dcc == 0), stop=(dcc == 15)))
                        p.tnop(s_pj)
                        p.wait("scalar", s_pj)
                        p.op("scalar", lambda s, d=dsts, h=h, c0=c0, pdst=pdst, sc=scale:
                             s.activation(d[h][:, c0:c0 + 512], pdst[:, :],
                                          AF.Copy, scale=sc), sem=s_pe)
                        if debug and kk < 2:
                            p.op("scalar", lambda s, kk=kk, pdst=pdst, sc=scale:
                                 s.activation(r01[:, kk * 512:(kk + 1) * 512], pdst[:, :],
                                              AF.Copy, scale=sc), sem=s_pe)
                        kk += 1
                if debug and sidx == 1:
                    p.wait("scalar", s_pj)
                    for kkx in range(2):
                        p.op("scalar", lambda s, kkx=kkx:
                             s.activation(r01[:, (2 + kkx) * 512:(3 + kkx) * 512],
                                          psQD[kkx][:, :], AF.Copy, scale=SC), sem=s_pe)
                for tcl in range(4):
                    tc = sidx * 4 + tcl
                    par = kk % 2
                    if kk >= 2:
                        p.wait("tensor", s_pe, s_pe.v - 1)
                    for dcc in range(16):
                        p.op("tensor", lambda t, tcl=tcl, dcc=dcc, sp=sp, par=par:
                             t.matmul(psP[par][:, :],
                                      xnst[sp][:, dcc * 512 + tcl * 128: dcc * 512 + (tcl + 1) * 128],
                                      wv[:, dcc * 512:(dcc + 1) * 512],
                                      start=(dcc == 0), stop=(dcc == 15)))
                    p.tnop(s_pj)
                    p.wait("scalar", s_pj)
                    p.op("scalar", lambda s, tc=tc, par=par:
                         s.copy(bass.AP(vTP, tc * 128, [[4 * S, 128], [S, 4], [1, 128]]),
                                psP[par][:, :]), sem=s_pe)
                    kk += 1
                p.op("tensor", lambda t: t.drain(), sem=s_xn)
            if debug:
                p.wait("sync", s_pe)
                p.dma("sync", dbg_r01[:, :], r01[:, :], s_ld)
                p.dma("sync", dbg_wq[:, :], wq[:, :], s_ld)
                for h in range(4):
                    p.dma("sync", dbg_qe[h * 128:(h + 1) * 128, :], qt[h][:, :], s_ld)
            p.wait("gpsimd", s_pe)
            if debug:
                p.wait("gpsimd", s_ld)
        zt = [_sb(ctx, nc, f"zt{h}", [128, S], BF16) for h in range(4)]
        wo = _sb(ctx, nc, "wo", [128, 4 * D], BF16)
        wostg = _sb(ctx, nc, "wostg", [128, D])
        p.wait("sync", s_pe)
        for e in range(4):
            if e >= 1:
                p.wait("sync", s_wo)
            p.dma("sync", wostg[:, :], WoT[e * 128:(e + 1) * 128, :], s_wo)
            p.wait("vector", s_wo)
            p.op("vector", lambda v, e=e:
                 v.tensor_copy(wo[:, e * D:(e + 1) * D], wostg[:, :]), sem=s_wo)

        p.wait("tensor", s_pe)
        nE = 0
        nM = 0
        n3 = 0
        for h in range(4):
            for sidx in range(8):
                q0 = sidx * 512
                ntk = 4 * sidx + 4
                for tkc in range(ntk):
                    par = (nE + 0) % 2
                    diag = (tkc >= 4 * sidx)
                    if nE >= 2:
                        p.wait("tensor", s_aE, nE - 1)
                    p.op("tensor", lambda t, h=h, tkc=tkc, q0=q0, par=par:
                         t.matmul(psP[par][:, :], kt[h][:, tkc * 128:(tkc + 1) * 128],
                                  qt[h][:, q0:q0 + 512], start=True, stop=True))
                    p.tnop(s_a1)
                    p.wait("scalar", s_a1)
                    if nE >= 2:
                        p.wait("scalar", s_a3, n3 - 1)
                    p.op("scalar", lambda s, par=par:
                         s.activation(Eb[par][:, :], psP[par][:, :], AF.Exp),
                         sem=s_aE)
                    nE += 1
                    if diag:
                        off = tkc - 4 * sidx
                        p.wait("vector", s_aE)
                        p.op("vector", lambda v, par=par, off=off:
                             v.tensor_mul(Eb[par][:, :], Eb[par][:, :],
                                          masks[:, off * 512:(off + 1) * 512]),
                             sem=s_aM)
                        nM += 1
                        p.wait("tensor", s_aM)
                    else:
                        p.wait("tensor", s_aE)
                    p.op("tensor", lambda t, par=par, first=(tkc == 0), last=(tkc == ntk - 1):
                         t.matmul(psDen[:, :], ones128[:, :], Eb[par][:, :],
                                  start=first, stop=last))
                    p.op("tensor", lambda t, h=h, tkc=tkc, par=par, first=(tkc == 0), last=(tkc == ntk - 1):
                         t.matmul(psZ[:, :], vTP[:, h * S + tkc * 128: h * S + (tkc + 1) * 128],
                                  Eb[par][:, :], start=first, stop=last),
                         sem=s_a3)
                    n3 += 1
                p.tnop(s_zn)
                p.wait("vector", s_zn)
                p.op("vector", lambda v: v.reciprocal(rec[:, :], psDen[:, :]), sem=s_zv)
                p.wait("vector", s_zv)
                p.op("vector", lambda v, h=h, q0=q0:
                     v.tensor_mul(zt[h][:, q0:q0 + 512], psZ[:, :], rec[:, :]),
                     sem=s_zv)
                p.wait("tensor", s_zv)

        p.wait("tensor", s_aE)
        p.wait("tensor", s_wo)
        ky = 0
        for dmc in range(16):
            for (c0, c1) in strips(S):
                par = ky % 2
                if ky >= 2:
                    p.wait("tensor", s_y2, s_y2.v - 33)
                for e in range(4):
                    p.op("tensor", lambda t, dmc=dmc, e=e, c0=c0, c1=c1, par=par:
                         t.matmul(psP[par][:, :],
                                  wo[:, e * D + dmc * 128: e * D + (dmc + 1) * 128],
                                  zt[e][:, c0:c1], start=(e == 0), stop=(e == 3)))
                p.tnop(s_y1)
                p.wait("vector", s_y1)
                if ky >= 2:
                    p.wait("vector", s_y2, s_y2.v - 17)
                p.op("vector", lambda v, par=par:
                     v.tensor_copy(ob[par][:, :], psP[par][:, :]), sem=s_y2)
                p.wait("sync", s_y2)
                qblk = c0 // NT
                cc0 = c0 - qblk * NT
                p.dma("sync", rs_in[qblk * D + dmc * 128: qblk * D + (dmc + 1) * 128,
                                    cc0:cc0 + (c1 - c0)], ob[par][:, :], s_y2)
                ky += 1
        if debug:
            p.wait("sync", s_y2)
            for h in range(4):
                p.dma("sync", dbg_q[h * 128:(h + 1) * 128, :], qt[h][:, :], s_st)
                p.dma("sync", dbg_k[h * 128:(h + 1) * 128, :], kt[h][:, :], s_st)
                p.dma("sync", dbg_z[h * 128:(h + 1) * 128, :], zt[h][:, :], s_st)
            p.dma("sync", dbg_v[:, :], vTP[:, :], s_st)
        p.wait("gpsimd", s_y2)
        defer = io.get("defer_rs")
        if defer is not None:
            p.op("gpsimd", lambda g: g.collective_compute(
                "ReduceScatter", ALU.add, replica_groups=[[0, 1, 2, 3], [4, 5, 6, 7]],
                ins=[rs_in.ap().opt()], outs=[out.ap().opt()]), sem=defer)
        else:
            p.op("gpsimd", lambda g: g.collective_compute(
                "ReduceScatter", ALU.add, replica_groups=[[0, 1, 2, 3], [4, 5, 6, 7]],
                ins=[rs_in.ap().opt()], outs=[rs_out.ap().opt()]), sem=s_cc)
            p.wait("sync", s_cc)
            p.dma("sync", out[:, :], rs_out[:, :], s_st)
            p.wait("sync", s_st)
        p.emit()
    return nc


def stage_attn0_weights(inputs, core):
    hg = core % 4
    hs = slice(4 * hg, 4 * hg + 4)
    WQ = np.asarray(inputs["L0_attn_W_Q"][hs])
    WK = np.asarray(inputs["L0_attn_W_K"][hs])
    WV = np.asarray(inputs["L0_attn_W_V"][hs])
    WO = np.asarray(inputs["L0_attn_W_O"][hs])
    WqT = np.ascontiguousarray(WQ.reshape(512, D).T)
    WkT = np.ascontiguousarray(WK.reshape(512, D).T)
    WvT = np.ascontiguousarray(WV.reshape(512, D).T)
    WoT = np.ascontiguousarray(WO.transpose(0, 2, 1).reshape(512, D))
    return WqT, WkT, WvT, WoT


def stage_masks():
    m = np.zeros((4, 128, 512), dtype=np.float32)
    for off in range(4):
        tq = np.arange(512)[None, :]
        tk = np.arange(128)[:, None]
        m[off] = (tq >= tk + off * 128 - 0).astype(np.float32) if False else \
            (tq >= (tk + off * 128)).astype(np.float32)
    return m.reshape(512, 512).astype(BF16NP)



def build_launch_D(nc=None, io=None):
    own = nc is None
    if own:
        nc = bass.Bass("TRN2", target_bir_lowering=False)
    io = io or {}
    xT = io.get("xT") or nc.dram_tensor("xT", [D, NT], F32, kind="ExternalInput")
    y0 = io.get("y0") or nc.dram_tensor("y0", [D, NT], BF16, kind="ExternalInput")
    f0 = io.get("f0") or nc.dram_tensor("f0", [D, NT], BF16, kind="ExternalInput")
    x1T = io.get("x1T") or nc.dram_tensor("x1T", [D, NT], F32, kind="ExternalOutput")
    xn1T = io.get("xn1T") or nc.dram_tensor("xn1T", [D, NT], BF16, kind="ExternalOutput")
    ctx = ExitStack()
    with ctx:
        p = Prog(nc, ctx)
        s_init = p.sem("init")
        p.make_consts(ctx, nc, s_init)
        p.ones_r = _sb(ctx, nc, "ones_r", [128, 1])
        p.ones_b = _sb(ctx, nc, "ones_b", [1, 128])
        p.op("vector", lambda v: v.memset(p.ones_r[:, 0:1], 1.0), sem=s_init)
        p.op("vector", lambda v: v.memset(p.ones_b[0:1, :], 1.0), sem=s_init)
        xt = [_sb(ctx, nc, f"xt{i}", [128, NT]) for i in range(16)]
        xnb = [_sb(ctx, nc, f"xnb{i}", [128, NT], BF16) for i in range(16)]
        s_ld = p.sem("ld")
        s_ad = p.sem("ad")
        s_out = p.sem("out")
        for (h, vv) in io.get("wait", []):
            self_w = None
            p.steps["sync"].append(lambda e, h=h, vv=vv: e.wait_ge(h, vv))
        for i in range(16):
            p.dma("sync", xt[i][:, :], xT[i * 128:(i + 1) * 128, :], s_ld)
        actx = ExitStack()
        with actx:
            ab = [_sb(actx, nc, f"ab{k}", [128, NT], BF16) for k in range(2)]
            for src in (y0, f0):
                for i in range(16):
                    par = i % 2
                    if not (src is y0 and i < 2):
                        p.wait("sync", s_ad)
                    p.dma("sync", ab[par][:, :], src[i * 128:(i + 1) * 128, :], s_ld)
                    p.wait("vector", s_ld)
                    if src is y0:
                        p.wait("vector", s_ld, s_ld.v)
                    p.op("vector", lambda v, i=i, par=par:
                         v.tensor_add(xt[i][:, :], xt[i][:, :], ab[par][:, :]),
                         sem=s_ad)
            p.wait("vector", s_ad)
        p.wait("sync", s_ad)
        for i in range(16):
            p.dma("sync", x1T[i * 128:(i + 1) * 128, :], xt[i][:, :], s_out)
        ready = [(e, s_ad) for e in ("tensor", "scalar", "vector")]
        s_ap = emit_ln(p, ctx, nc, "ln", [t[:, :] for t in xt],
                       [t[:, :] for t in xnb], NT, ready)
        p.wait("sync", s_ap)
        for i in range(16):
            p.dma("sync", xn1T[i * 128:(i + 1) * 128, :], xnb[i][:, :], s_out)
        p.wait("sync", s_out)
        p.emit()
    return nc



def build_launch_G(nc=None, io=None):
    own = nc is None
    if own:
        nc = bass.Bass("TRN2", target_bir_lowering=False)
    io = io or {}
    x1 = io.get("x1T") or nc.dram_tensor("x1T", [D, NT], F32, kind="ExternalInput")
    y1 = io.get("y1") or nc.dram_tensor("y1", [D, NT], BF16, kind="ExternalInput")
    f1 = io.get("f1") or nc.dram_tensor("f1", [D, NT], BF16, kind="ExternalInput")
    lnw = nc.dram_tensor("lnw", [D, 1], F32, kind="ExternalInput")
    lnb = nc.dram_tensor("lnb", [D, 1], F32, kind="ExternalInput")
    x2T = nc.dram_tensor("x2T", [D, NT], F32, kind="ExternalOutput")
    ctx = ExitStack()
    with ctx:
        p = Prog(nc, ctx)
        s_init = p.sem("init")
        p.make_consts(ctx, nc, s_init)
        p.ones_r = _sb(ctx, nc, "ones_r", [128, 1])
        p.ones_b = _sb(ctx, nc, "ones_b", [1, 128])
        p.op("vector", lambda v: v.memset(p.ones_r[:, 0:1], 1.0), sem=s_init)
        p.op("vector", lambda v: v.memset(p.ones_b[0:1, :], 1.0), sem=s_init)
        yt = [_sb(ctx, nc, f"yt{i}", [128, NT]) for i in range(16)]
        gw = _sb(ctx, nc, "gw", [128, 16])
        gb = _sb(ctx, nc, "gb", [128, 16])
        s_ld = p.sem("ld")
        s_ad = p.sem("ad")
        s_out = p.sem("out")
        for (h, vv) in io.get("wait", []):
            p.steps["sync"].append(lambda e, h=h, vv=vv: e.wait_ge(h, vv))
        ybf = _sb(ctx, nc, "ybfG", [128, NT], BF16)
        s_yc = p.sem("yc")
        for i in range(16):
            if i >= 1:
                p.wait("sync", s_yc)
            p.dma("sync", ybf[:, :], y1[i * 128:(i + 1) * 128, :], s_ld)
            p.wait("vector", s_ld)
            p.op("vector", lambda v, i=i: v.tensor_copy(yt[i][:, :], ybf[:, :]),
                 sem=s_yc)
            p.dma("sync", gw[:, i:i + 1], lnw[i * 128:(i + 1) * 128, :], s_ld)
            p.dma("sync", gb[:, i:i + 1], lnb[i * 128:(i + 1) * 128, :], s_ld)
        p.wait("vector", s_yc)
        p.wait("tensor", s_yc)
        p.wait("scalar", s_yc)
        ready = [(e, s_ld) for e in ("tensor", "scalar", "vector")]
        s_ap = emit_ln(p, ctx, nc, "ln", [t[:, :] for t in yt],
                       [t[:, :] for t in yt], NT, ready,
                       gain=[gw[:, i:i + 1] for i in range(16)],
                       bias=[gb[:, i:i + 1] for i in range(16)])
        actx = ExitStack()
        with actx:
            ab = [_sb(actx, nc, f"ab{k}", [128, NT]) for k in range(2)]
            abb = [_sb(actx, nc, f"abb{k}", [128, NT], BF16) for k in range(2)]
            for src in (x1, f1):
                bufs = ab if src is x1 else abb
                for i in range(16):
                    par = i % 2
                    if not (src is x1 and i < 2):
                        p.wait("sync", s_ad)
                    p.dma("sync", bufs[par][:, :], src[i * 128:(i + 1) * 128, :], s_ld)
                    p.wait("vector", s_ld)
                    if src is x1:
                        p.wait("vector", s_ap)
                    p.op("vector", lambda v, i=i, par=par, bufs=bufs:
                         v.tensor_add(yt[i][:, :], yt[i][:, :], bufs[par][:, :]),
                         sem=s_ad)
            p.wait("vector", s_ad)
        p.wait("sync", s_ad)
        for i in range(16):
            p.dma("sync", x2T[i * 128:(i + 1) * 128, :], yt[i][:, :], s_out)
        p.wait("sync", s_out)
        p.emit()
    return nc



def build_launch_F(nc=None, io=None):
    own = nc is None
    if own:
        nc = bass.Bass("TRN2", target_bir_lowering=False)
    io = io or {}
    xnF = io.get("xnF")
    if xnF is None:
        xnF = (nc.dram_tensor("xnF1", [D, T], BF16, kind="ExternalInput"), "flat")
    gwT = nc.dram_tensor("gwT", [D, NE], F32, kind="ExternalInput")
    w1T = [nc.dram_tensor(f"w1T{e}", [D, 2 * IE], F32, kind="ExternalInput") for e in range(2)]
    w2T = [nc.dram_tensor(f"w2T{e}", [IE, D], F32, kind="ExternalInput") for e in range(2)]
    sw1gT = nc.dram_tensor("sw1gT", [D, 352], F32, kind="ExternalInput")
    sw1uT = nc.dram_tensor("sw1uT", [D, 352], F32, kind="ExternalInput")
    sw2T = nc.dram_tensor("sw2T", [352, D], F32, kind="ExternalInput")
    ident = nc.dram_tensor("identF", [128, 128], F32, kind="ExternalInput")
    out = io.get("ffn1sum") or nc.dram_tensor("ffn1sum", [D, NT], BF16, kind="ExternalOutput")
    rs_in = nc.dram_tensor("rs_inF", [8 * D, NT], BF16)
    rs_out = nc.dram_tensor("rs_outF", [D, NT], BF16)
    w2bf = nc.dram_tensor("w2bf", [16, 128, 25 * 128], BF16)
    w1bf = nc.dram_tensor("w1bf", [25, 2, 128, 16 * 128], BF16)
    TSB = 2048
    NBLK = 25

    ctx = ExitStack()
    with ctx:
        p = Prog(nc, ctx)
        s_init = p.sem("init")
        p.make_consts(ctx, nc, s_init)
        p.warmup_scalar(ctx, nc, [AF.Exp, AF.Silu], s_init)
        ones_bb = _sb(ctx, nc, "ones_bb", [1, 128], BF16)
        p.op("vector", lambda v: v.memset(ones_bb[0:1, :], 1.0), sem=s_init)
        ids = _sb(ctx, nc, "ids", [128, 128])
        comb01d = nc.dram_tensor("comb01d", [1, 2 * T], BF16)
        combsl = _sb(ctx, nc, "combsl", [1, 2 * TSB], BF16)
        s_ld = p.sem("ld")
        s_r1 = p.sem("r1")
        s_r2 = p.sem("r2")
        s_r3 = p.sem("r3")
        s_rv = p.sem("rv")
        s_r4 = p.sem("r4")
        s_w2 = p.sem("w2")
        s_x = p.sem("x")
        s_wl = p.sem("wl")
        s_g1 = p.sem("g1")
        s_g2 = p.sem("g2")
        s_g3 = p.sem("g3")
        s_d1 = p.sem("d1")
        s_d2 = p.sem("d2")
        s_cc = p.sem("cc")
        s_st = p.sem("st")
        p.dma("sync", ids[:, :], ident[:, :], s_ld)

        rctx = ExitStack()
        with rctx:
            gww = _sb(rctx, nc, "gww", [128, 16 * NE], BF16)
            gwst = _sb(rctx, nc, "gwst", [128, 16 * NE])
            xs2 = [_sb(rctx, nc, f"xs2{k}", [128, 16 * 512], BF16) for k in range(2)]
            E16 = _sb(rctx, nc, "E16", [NE, 512])
            Etok = _sb(rctx, nc, "Etok", [128, 64])
            cur = _sb(rctx, nc, "cur", [128, 16])
            mx = _sb(rctx, nc, "mx", [128, 1])
            ge = _sb(rctx, nc, "ge", [128, 16])
            sel = _sb(rctx, nc, "sel", [128, 16])
            c1 = _sb(rctx, nc, "c1", [128, 16])
            rs_ = _sb(rctx, nc, "rs_", [128, 1])
            rc = _sb(rctx, nc, "rc", [128, 1])
            comb = _sb(rctx, nc, "comb", [128, 64])
            psL = _ps(rctx, nc, "psL", [128, 512])
            psT = _ps(rctx, nc, "psT", [128, 64])
            psB2 = _ps(rctx, nc, "psB2", [1, 1024])
            combsb = _sb(rctx, nc, "combsb", [1, 1024], BF16)
            for i in range(16):
                p.dma("sync", gwst[:, i * NE:(i + 1) * NE], gwT[i * 128:(i + 1) * 128, :], s_ld)
            p.wait("vector", s_ld)
            p.op("vector", lambda v: v.tensor_copy(gww[:, :], gwst[:, :]), sem=s_ld)
            for st in range(16):
                sp = st % 2
                c0 = st * 512
                if st >= 2:
                    p.wait("sync", s_r1, s_r1.v - 1)
                p.dma("sync", xs2[sp][:, :], tok_ap_multi(xnF, c0, 512), s_x)
                p.wait("tensor", s_x)
                p.wait("tensor", s_ld)
                if st >= 2:
                    p.wait("tensor", s_r2, s_r2.v - 1)
                for i in range(16):
                    p.op("tensor", lambda t, i=i, sp=sp:
                         t.matmul(psL[0:NE, :], gww[:, i * NE:(i + 1) * NE],
                                  xs2[sp][:, i * 512:(i + 1) * 512],
                                  start=(i == 0), stop=(i == 15)))
                p.tnop(s_r1)
                p.wait("scalar", s_r1)
                p.op("scalar", lambda sc: sc.activation(E16[:, :], psL[0:NE, :], AF.Exp),
                     sem=s_r2)
                p.wait("tensor", s_r2)
                for j in range(4):
                    p.op("tensor", lambda t, j=j:
                         t.transpose(psT[:, j * 16:(j + 1) * 16],
                                     E16[0:NE, j * 128:(j + 1) * 128], ids[0:NE, 0:NE]))
                p.tnop(s_r3)
                p.wait("vector", s_r3)
                p.op("vector", lambda v: v.tensor_copy(Etok[:, :], psT[:, :]), sem=s_rv)
                for j in range(4):
                    p.wait("vector", s_rv)
                    p.op("vector", lambda v, j=j:
                         v.tensor_copy(cur[:, :], Etok[:, j * 16:(j + 1) * 16]), sem=s_rv)
                    for rnd in range(6):
                        p.wait("vector", s_rv)
                        p.op("vector", lambda v: v.reduce_max(mx[:, 0:1], cur[:, :], axis=AX.X),
                             sem=s_rv)
                        if rnd < 5:
                            p.wait("vector", s_rv)
                            p.op("vector", lambda v:
                                 v.tensor_scalar(ge[:, :], cur[:, :], mx[:, 0:1], None, ALU.is_ge),
                                 sem=s_rv)
                            p.wait("vector", s_rv)
                            p.op("vector", lambda v:
                                 v.scalar_tensor_tensor(cur[:, :], ge[:, :], -1e30, cur[:, :],
                                                        ALU.mult, ALU.add), sem=s_rv)
                    p.wait("vector", s_rv)
                    p.op("vector", lambda v, j=j:
                         v.tensor_scalar(sel[:, :], Etok[:, j * 16:(j + 1) * 16], mx[:, 0:1],
                                         None, ALU.is_ge), sem=s_rv)
                    p.op("vector", lambda v, j=j:
                         v.reduce_sum(rs_[:, 0:1], Etok[:, j * 16:(j + 1) * 16], axis=AX.X),
                         sem=s_rv)
                    p.wait("vector", s_rv)
                    p.op("vector", lambda v, j=j:
                         v.tensor_mul(c1[:, :], sel[:, :], Etok[:, j * 16:(j + 1) * 16]),
                         sem=s_rv)
                    p.op("vector", lambda v: v.reciprocal(rc[:, 0:1], rs_[:, 0:1]), sem=s_rv)
                    p.wait("vector", s_rv)
                    p.op("vector", lambda v, j=j:
                         v.tensor_scalar(comb[:, j * 16:(j + 1) * 16], c1[:, :], rc[:, 0:1],
                                         None, ALU.mult), sem=s_rv)
                p.wait("tensor", s_rv)
                if st >= 1:
                    p.wait("tensor", s_r4, s_r4.v - 32)
                for j in range(4):
                    for er in range(2):
                        p.op("tensor", lambda t, j=j, er=er:
                             t.transpose(psB2[0:1, er * 512 + j * 128: er * 512 + (j + 1) * 128],
                                         comb[:, j * 16 + er: j * 16 + er + 1], ids[:, :]))
                p.tnop(s_r3)
                p.wait("vector", s_r3)
                p.op("vector", lambda v: v.tensor_copy(combsb[0:1, :], psB2[0:1, :]),
                     sem=s_r4)
                p.wait("sync", s_r4)
                for er in range(2):
                    p.dma("sync", comb01d[0:1, er * T + c0: er * T + c0 + 512],
                          combsb[0:1, er * 512:(er + 1) * 512], s_r4)

        wctx = ExitStack()
        with wctx:
            wst = _sb(wctx, nc, "wst", [128, 16 * 128])
            wbf = _sb(wctx, nc, "wbf", [128, 16 * 128], BF16)
            s_pc = p.sem("pc")
            for blk in range(NBLK):
                if blk < 22:
                    e, ic = blk // 11, blk % 11
                    src = w2T[e]
                    r0 = ic * 128
                    nr = 128
                else:
                    src = sw2T
                    r0 = (blk - 22) * 128
                    nr = 128 if blk < 24 else 96
                if blk >= 1:
                    p.wait("sync", s_pc)
                p.dma("sync", wst[0:nr, :], src[r0:r0 + nr, :], s_ld)
                p.wait("vector", s_ld)
                if blk >= 1:
                    p.wait("vector", s_pc)
                p.op("vector", lambda v, nr=nr: v.tensor_copy(wbf[0:nr, :], wst[0:nr, :]),
                     sem=s_pc)
                if blk == 24:
                    p.op("vector", lambda v: v.memset(wbf[96:128, :], 0.0), sem=s_pc)
                p.wait("sync", s_pc)
                p.dma("sync",
                      bass.AP(w2bf, blk * 128, [[25 * 128, 128], [128 * 25 * 128, 16], [1, 128]]),
                      wbf[:, :], s_pc)
            for blk in range(NBLK):
                if blk < 22:
                    e, ic = blk // 11, blk % 11
                    srcs = ((w1T[e], ic * 128, 128), (w1T[e], IE + ic * 128, 128))
                else:
                    isc = blk - 22
                    ng = 128 if blk < 24 else 96
                    srcs = ((sw1gT, isc * 128, ng), (sw1uT, isc * 128, ng))
                for j, (srcT, cc0, ng) in enumerate(srcs):
                    p.wait("sync", s_pc)
                    if ng < 128:
                        p.wait("vector", s_pc)
                        p.op("vector", lambda v: v.memset(wst[:, :], 0.0), sem=s_pc)
                        p.wait("sync", s_pc)
                    Trows = srcT.shape[1]
                    p.dma("sync",
                          bass.AP(wst, 0, [[2048, 128], [128, 16], [1, ng]]),
                          bass.AP(srcT, cc0, [[Trows, 128], [128 * Trows, 16], [1, ng]]),
                          s_ld)
                    p.wait("vector", s_ld)
                    p.wait("vector", s_pc)
                    p.op("vector", lambda v: v.tensor_copy(wbf[:, :], wst[:, :]), sem=s_pc)
                    p.wait("sync", s_pc)
                    p.dma("sync", w1bf[blk, j, :, :], wbf[:, :], s_pc)
            p.wait("sync", s_pc)

        xnt = _sb(ctx, nc, "xnt", [128, 16 * TSB], BF16)
        actb = _sb(ctx, nc, "actb", [128, NBLK * TSB], BF16)
        w1g = [_sb(ctx, nc, f"w1g{k}", [128, 16 * 128], BF16) for k in range(2)]
        w1u = [_sb(ctx, nc, f"w1u{k}", [128, 16 * 128], BF16) for k in range(2)]
        wstg = [_sb(ctx, nc, "wstgF0", [128, 16 * 128])] * 2
        silg = [_sb(ctx, nc, f"silgF{k}", [128, 512], BF16) for k in range(2)]
        w2dm = [_sb(ctx, nc, "w2dm0", [128, NBLK * 128], BF16)] * 2
        ob = [_sb(ctx, nc, f"obF{k}", [128, 512], BF16) for k in range(2)]
        psG = [_ps(ctx, nc, f"psGF{k}", [128, 512]) for k in range(2)]
        psU = [_ps(ctx, nc, f"psUF{k}", [128, 512]) for k in range(2)]
        psC = [_ps(ctx, nc, f"psCF{k}", [128, 512]) for k in range(2)]
        psD = [_ps(ctx, nc, f"psDF{k}", [128, 512]) for k in range(2)]

        s_wc = p.sem("wc")
        p.wait("tensor", s_r4)
        k1 = 0
        k2 = 0
        kwl = 0
        for tsb in range(4):
            t0 = tsb * TSB
            p.wait("sync", s_g3)
            p.dma("sync",
                  bass.AP(xnt, 0, [[16 * TSB, 128], [TSB, 16], [1, NT]]),
                  tok_ap_multi(xnF, t0, NT), s_x)
            p.dma("sync",
                  bass.AP(xnt, NT, [[16 * TSB, 128], [TSB, 16], [1, NT]]),
                  tok_ap_multi(xnF, t0 + NT, NT), s_x)
            if tsb >= 1:
                p.wait("sync", s_g1)
            for er in range(2):
                p.dma("sync", combsl[0:1, er * TSB:(er + 1) * TSB],
                      comb01d[0:1, er * T + t0: er * T + t0 + TSB], s_x)
            for blk in range(NBLK):
                wpar = kwl % 2
                kwl += 1
                if kwl >= 2:
                    p.wait("sync", s_g1, s_g1.v - 4)
                p.dma("sync", w1g[wpar][:, :], w1bf[blk, 0, :, :], s_wl)
                p.dma("sync", w1u[wpar][:, :], w1bf[blk, 1, :, :], s_wl)
                p.wait("tensor", s_wl)
                p.wait("tensor", s_x)
                for ts in range(4):
                    c0 = ts * 512
                    par = k1 % 2
                    if k1 >= 2:
                        p.wait("tensor", s_g3, s_g3.v - 1)
                    for dcc in range(16):
                        p.op("tensor", lambda t, dcc=dcc, c0=c0, par=par, wpar=wpar:
                             t.matmul(psG[par][:, :], w1g[wpar][:, dcc * 128:(dcc + 1) * 128],
                                      xnt[:, dcc * TSB + c0: dcc * TSB + c0 + 512],
                                      start=(dcc == 0), stop=(dcc == 15)))
                    for dcc in range(16):
                        p.op("tensor", lambda t, dcc=dcc, c0=c0, par=par, wpar=wpar:
                             t.matmul(psU[par][:, :], w1u[wpar][:, dcc * 128:(dcc + 1) * 128],
                                      xnt[:, dcc * TSB + c0: dcc * TSB + c0 + 512],
                                      start=(dcc == 0), stop=(dcc == 15)))
                    if blk < 22:
                        er = blk // 11
                        p.op("tensor", lambda t, c0=c0, par=par, er=er:
                             t.matmul(psC[par][:, :], ones_bb[0:1, :],
                                      combsl[0:1, er * TSB + c0: er * TSB + c0 + 512],
                                      start=True, stop=True))
                    p.tnop(s_g1)
                    p.wait("scalar", s_g1)
                    if k1 >= 2:
                        p.wait("scalar", s_g3, s_g3.v - 1)
                    p.op("scalar", lambda sc, par=par:
                         sc.activation(silg[par][:, :], psG[par][:, :], AF.Silu), sem=s_g2)
                    p.wait("vector", s_g2)
                    if blk < 22:
                        p.op("vector", lambda v, par=par:
                             v.tensor_mul(silg[par][:, :], silg[par][:, :], psC[par][:, :]),
                             sem=s_g2)
                        p.wait("vector", s_g2)
                    p.op("vector", lambda v, blk=blk, c0=c0, par=par:
                         v.tensor_mul(actb[:, blk * TSB + c0: blk * TSB + c0 + 512],
                                      silg[par][:, :], psU[par][:, :]), sem=s_g3)
                    k1 += 1
            p.wait("tensor", s_g3)
            for dmc in range(16):
                parw = 0
                if dmc >= 1:
                    p.wait("sync", s_d2, s_d2.v - 16)
                p.dma("sync", w2dm[parw][:, :], w2bf[dmc, :, :], s_wl)
                p.wait("tensor", s_wl)
                for ts in range(4):
                    c0 = ts * 512
                    par = k2 % 2
                    if k2 >= 2:
                        p.wait("tensor", s_d2, s_d2.v - 33)
                    for blk in range(NBLK):
                        p.op("tensor", lambda t, blk=blk, dmc=dmc, c0=c0, par=par, parw=parw:
                             t.matmul(psD[par][:, :], w2dm[parw][:, blk * 128:(blk + 1) * 128],
                                      actb[:, blk * TSB + c0: blk * TSB + c0 + 512],
                                      start=(blk == 0), stop=(blk == NBLK - 1)))
                    p.tnop(s_d1)
                    p.wait("vector", s_d1)
                    if k2 >= 2:
                        p.wait("vector", s_d2, s_d2.v - 17)
                    p.op("vector", lambda v, par=par:
                         v.tensor_copy(ob[par][:, :], psD[par][:, :]), sem=s_d2)
                    p.wait("sync", s_d2)
                    rb = 2 * tsb + ts // 2
                    cc = (ts % 2) * 512
                    p.dma("sync", rs_in[rb * D + dmc * 128: rb * D + (dmc + 1) * 128,
                                        cc:cc + 512], ob[par][:, :], s_d2)
                    k2 += 1
        p.wait("gpsimd", s_d2)
        p.op("gpsimd", lambda g: g.collective_compute(
            "ReduceScatter", ALU.add, replica_groups=[list(range(8))],
            ins=[rs_in.ap().opt()], outs=[rs_out.ap().opt()]), sem=s_cc)
        p.wait("sync", s_cc)
        p.dma("sync", out[:, :], rs_out[:, :], s_st)
        p.wait("sync", s_st)
        p.emit()
    return nc



def build_launch_E(debug=False, upto="full", nc=None, io=None):
    own = nc is None
    if own:
        nc = bass.Bass("TRN2", target_bir_lowering=False)
    io = io or {}
    xn1B = io.get("xn1B")
    if xn1B is None:
        xn1B = (nc.dram_tensor("xn1B", [D, S], BF16, kind="ExternalInput"), "flat")
    WrT = nc.dram_tensor("E_WrT", [D, 512], F32, kind="ExternalInput")
    WkT = nc.dram_tensor("E_WkT", [D, 512], F32, kind="ExternalInput")
    WvT = nc.dram_tensor("E_WvT", [D, 512], F32, kind="ExternalInput")
    WgT = nc.dram_tensor("E_WgT", [D, 512], F32, kind="ExternalInput")
    WoT = nc.dram_tensor("E_WoT", [512, D], F32, kind="ExternalInput")
    tdh = nc.dram_tensor("tdh", [128, 4], F32, kind="ExternalInput")
    tfh = nc.dram_tensor("tfh", [128, 4], F32, kind="ExternalInput")
    tmk = nc.dram_tensor("tmk", [D, 1], F32, kind="ExternalInput")
    tmv = nc.dram_tensor("tmv", [D, 1], F32, kind="ExternalInput")
    tmr = nc.dram_tensor("tmr", [D, 1], F32, kind="ExternalInput")
    tmg = nc.dram_tensor("tmg", [D, 1], F32, kind="ExternalInput")
    exps = nc.dram_tensor("exps", [128, 1040], F32, kind="ExternalInput")
    identE = nc.dram_tensor("identE", [128, 128], BF16, kind="ExternalInput")
    out = io.get("y1sum") or nc.dram_tensor("y1sum", [D, NT], BF16, kind="ExternalOutput")
    ogd = nc.dram_tensor("ogd", [512, S], BF16)
    rs_in = nc.dram_tensor("rs_inE", [4 * D, NT], BF16)
    rs_out = nc.dram_tensor("rs_outE", [D, NT], BF16)
    if debug:
        dbg_r = nc.dram_tensor("dbg_r", [128, 4 * S], BF16, kind="ExternalOutput")
        dbg_k = nc.dram_tensor("dbg_k", [128, 4 * S], BF16, kind="ExternalOutput")
        dbg_v = nc.dram_tensor("dbg_v", [128, 4 * S], BF16, kind="ExternalOutput")
        dbg_g = nc.dram_tensor("dbg_g", [128, 4 * S], BF16, kind="ExternalOutput")
        dbg_og = nc.dram_tensor("dbg_og", [512, S], BF16, kind="ExternalOutput")

    ctx = ExitStack()
    with ctx:
        p = Prog(nc, ctx)
        s_init = p.sem("init")
        p.make_consts(ctx, nc, s_init)
        p.warmup_scalar(ctx, nc, [AF.Copy, AF.Sigmoid, AF.Exp], s_init)
        identb = _sb(ctx, nc, "identb", [128, 128], BF16)
        ones128b = _sb(ctx, nc, "ones128b", [128, 128], BF16)
        zrow = _sb(ctx, nc, "zrow", [1, 128], BF16)
        expb = _sb(ctx, nc, "expb", [128, 1040])
        tdb = _sb(ctx, nc, "tdb", [128, 4])
        tfb = _sb(ctx, nc, "tfb", [128, 4])
        tmb = _sb(ctx, nc, "tmb", [128, 4 * 16])
        p.op("vector", lambda v: v.memset(ones128b[:, :], 1.0), sem=s_init)
        p.op("vector", lambda v: v.memset(zrow[0:1, :], 0.0), sem=s_init)
        rfp = _sb(ctx, nc, "rfp", [128, 4 * S], BF16)
        kfp = _sb(ctx, nc, "kfp", [128, 4 * S], BF16)
        vfp = _sb(ctx, nc, "vfp", [128, 4 * S], BF16)
        gsig = _sb(ctx, nc, "gsig", [128, 4 * S], BF16)

        s_ld = p.sem("ld")
        s_pj = p.sem("pj")
        s_pe = p.sem("pe")
        s_xc = p.sem("xc")
        s_xn = p.sem("xnc")
        s_wc = p.sem("wcE")

        p.dma("sync", identb[:, :], identE[:, :], s_ld)
        p.dma("sync", expb[:, :], exps[:, :], s_ld)
        p.dma("sync", tdb[:, :], tdh[:, :], s_ld)
        p.dma("sync", tfb[:, :], tfh[:, :], s_ld)
        for (j, tsrc) in enumerate((tmk, tmv, tmr, tmg)):
            for i in range(16):
                p.dma("sync", tmb[:, j * 16 + i: j * 16 + i + 1],
                      tsrc[i * 128:(i + 1) * 128, :], s_ld)

        pctx = ExitStack()
        with pctx:
            wst = _sb(pctx, nc, "wstE", [128, 8 * 512])
            wbb = _sb(pctx, nc, "wbbE", [128, 16 * 512], BF16)
            xbuf = [_sb(pctx, nc, "xbufE0", [128, 16 * 513], BF16)] * 2
            xstr = [_sb(pctx, nc, "xstrE0", [128, 16 * 512], BF16)] * 2
            xxb = _sb(pctx, nc, "xxbE", [128, 8 * 512], BF16)
            psP = [_ps(pctx, nc, f"psPE{k}", [128, 512]) for k in range(2)]
            kk = 0
            for (ip, (wsrc, dst, tmj, actf)) in enumerate((
                    (WrT, rfp, 2, AF.Copy), (WkT, kfp, 0, AF.Copy),
                    (WvT, vfp, 1, AF.Copy), (WgT, gsig, 3, AF.Sigmoid))):
                p.wait("sync", s_pe)
                for half in range(2):
                    if half == 1:
                        p.wait("sync", s_wc)
                    p.dma("sync", wst[:, :],
                          bass.AP(wsrc, half * 8 * 128 * 512,
                                  [[512, 128], [128 * 512, 8], [1, 512]]), s_ld)
                    p.wait("vector", s_ld)
                    p.op("vector", lambda v, half=half:
                         v.tensor_copy(wbb[:, half * 4096:(half + 1) * 4096],
                                       wst[:, :]), sem=s_wc)
                p.wait("tensor", s_wc)
                for ts in range(8):
                    sp = ts % 2
                    c0 = ts * 512
                    if kk >= 1:
                        p.wait("sync", s_xn)
                    if ts == 0:
                        p.dma("sync",
                              bass.AP(xbuf[sp], 1, [[16 * 513, 128], [513, 16], [1, 512]]),
                              tok_ap_multi(xn1B, 0, 512), s_ld)
                    elif xn1B[1] == "rank" and (c0 - 1) // NT != (c0 + 511) // NT:
                        p.dma("sync",
                              bass.AP(xbuf[sp], 1, [[16 * 513, 128], [513, 16], [1, 512]]),
                              tok_ap_multi(xn1B, c0, 512), s_ld)
                        p.dma("sync",
                              bass.AP(xbuf[sp], 0, [[16 * 513, 128], [513, 16], [1, 1]]),
                              tok_ap_multi(xn1B, c0 - 1, 1), s_ld, slow=True)
                    else:
                        p.dma("sync",
                              bass.AP(xbuf[sp], 0, [[16 * 513, 128], [513, 16], [1, 513]]),
                              tok_ap_multi(xn1B, c0 - 1, 513), s_ld)
                    p.wait("vector", s_ld)
                    if ts == 0:
                        for i in range(16):
                            p.op("vector", lambda v, sp=sp, i=i:
                                 v.tensor_copy(xbuf[sp][:, i * 513: i * 513 + 1],
                                               xbuf[sp][:, i * 513 + 2: i * 513 + 3]),
                                 sem=s_xc)
                        p.wait("vector", s_xc)
                    if kk >= 1:
                        p.wait("vector", s_xn)
                    for hh in range(2):
                        o8 = hh * 8 * 513
                        p.op("vector", lambda v, sp=sp, o8=o8:
                             v.tensor_sub(
                                 xxb[:, :],
                                 bass.AP(xbuf[sp], o8, [[16 * 513, 128], [513, 8], [1, 512]]),
                                 bass.AP(xbuf[sp], o8 + 1, [[16 * 513, 128], [513, 8], [1, 512]])),
                             sem=s_xc)
                        p.wait("vector", s_xc)
                        for ii in range(8):
                            i = hh * 8 + ii
                            p.op("vector", lambda v, sp=sp, i=i, ii=ii, tmj=tmj:
                                 v.scalar_tensor_tensor(
                                     xstr[sp][:, i * 512:(i + 1) * 512],
                                     xxb[:, ii * 512:(ii + 1) * 512],
                                     tmb[:, tmj * 16 + i: tmj * 16 + i + 1],
                                     xbuf[sp][:, i * 513 + 1:(i + 1) * 513],
                                     ALU.mult, ALU.add), sem=s_xc)
                        p.wait("vector", s_xc)
                    p.wait("tensor", s_xc)
                    for cc in range(4):
                        par = kk % 2
                        if kk >= 2:
                            p.wait("tensor", s_pe, s_pe.v - 1)
                        for dcc in range(16):
                            p.op("tensor", lambda t, cc=cc, dcc=dcc, sp=sp, par=par:
                                 t.matmul(psP[par][:, :],
                                          wbb[:, dcc * 512 + cc * 128: dcc * 512 + (cc + 1) * 128],
                                          xstr[sp][:, dcc * 512:(dcc + 1) * 512],
                                          start=(dcc == 0), stop=(dcc == 15)))
                        p.tnop(s_pj)
                        p.wait("scalar", s_pj)
                        p.op("scalar", lambda sc, dst=dst, cc=cc, c0=c0, par=par, actf=actf:
                             sc.activation(dst[:, cc * S + c0: cc * S + c0 + 512],
                                           psP[par][:, :], actf), sem=s_pe)
                        kk += 1
                    p.op("tensor", lambda t: t.drain(), sem=s_xn)
            p.wait("sync", s_pe)
            p.wait("gpsimd", s_pe)

        if debug:
            p.wait("sync", s_pe)
            s_dbg = p.sem("dbg")
            p.dma("sync", dbg_r[:, :], rfp[:, :], s_dbg)
            p.dma("sync", dbg_k[:, :], kfp[:, :], s_dbg)
            p.dma("sync", dbg_v[:, :], vfp[:, :], s_dbg)
            p.dma("sync", dbg_g[:, :], gsig[:, :], s_dbg)
            p.wait("sync", s_dbg)

        if upto == "proj":
            s_end = p.sem("endX")
            p.wait("sync", s_pe)
            p.dma("sync", out[:, :], rs_out[:, :], s_end)
            p.wait("sync", s_end)
            p.emit()
            return nc
        wkvctx = ExitStack()
        kctx = wkvctx.__enter__() or wkvctx
        TBL = _sb(wkvctx, nc, "TBL", [128, 8 * 128], BF16)
        tscr = _sb(wkvctx, nc, "tscr", [128, 128])
        ldb = _sb(wkvctx, nc, "ldb", [128, 2])
        wdp = _sb(wkvctx, nc, "wdp", [128, 16])
        Sst = _sb(wkvctx, nc, "Sst", [128, 128])
        Sbf = _sb(wkvctx, nc, "Sbf", [128, 128], BF16)
        oacc = _sb(wkvctx, nc, "oacc", [128, S])
        pbuf = _sb(wkvctx, nc, "pbuf", [128, 3840], BF16)
        pbufU = _sb(wkvctx, nc, "pbufU", [128, S], BF16)
        tmpb = _sb(wkvctx, nc, "tmpb", [128, 512])
        scb = _sb(wkvctx, nc, "scb", [128, 8 * 2048], BF16)
        Ab = _sb(wkvctx, nc, "Ab", [128, 128], BF16)
        vtp = _sb(wkvctx, nc, "vtp", [128, 128], BF16)
        ktp = _sb(wkvctx, nc, "ktp", [128, 128], BF16)
        otb = _sb(wkvctx, nc, "otb", [128, 128], BF16)
        t1b = _sb(wkvctx, nc, "t1b", [128, 128])
        ogb = _sb(wkvctx, nc, "ogb", [128, 128], BF16)
        psA = _ps(wkvctx, nc, "psA", [128, 128])
        psO = _ps(wkvctx, nc, "psO", [128, 128])
        psKV = _ps(wkvctx, nc, "psKV", [128, 128])
        psT2 = _ps(wkvctx, nc, "psT2", [128, 128], BF16)
        psCl = _ps(wkvctx, nc, "psCl", [128, 512])

        s_tb = p.sem("tb")
        s_lg = p.sem("lg")
        s_lgm = p.sem("lgm")
        s_ck = p.sem("ck")
        s_cm = p.sem("cm")
        s_sp2 = p.sem("sp2")

        SCB = {"KB1": 0, "KB2": 1, "KB3": 2, "K16": 3, "R32": 4, "R16": 5,
               "R128": 6, "K128": 7}
        EXPB = {"R128": 0, "K128": 1, "R32": 2, "KB1": 3, "KB2": 4, "KB3": 5,
                "R16": 6, "K16": 7}

        for h in range(4):
            hb = h * S
            p.wait("scalar", s_pe)
            p.op("scalar", lambda sc, h=h:
                 sc.activation(ldb[:, 0:1], tdb[:, h:h + 1], AF.Exp), sem=s_tb)
            p.wait("vector", s_tb)
            p.op("vector", lambda v:
                 v.tensor_scalar(ldb[:, 1:2], ldb[:, 0:1], -1.0, None, ALU.mult),
                 sem=s_tb)
            p.wait("vector", s_tb)
            for name, eb in EXPB.items():
                tb = SCB[name]
                p.op("vector", lambda v, eb=eb:
                     v.tensor_scalar(tscr[:, :], expb[:, eb * 128:(eb + 1) * 128],
                                     ldb[:, 1:2], None, ALU.mult), sem=s_tb)
                p.wait("scalar", s_tb)
                p.op("scalar", lambda sc, tb=tb:
                     sc.activation(TBL[:, tb * 128:(tb + 1) * 128], tscr[:, :],
                                   AF.Exp), sem=s_tb)
                p.wait("vector", s_tb)
            p.op("vector", lambda v:
                 v.tensor_scalar(tscr[:, 0:16], expb[:, 1024:1040], ldb[:, 1:2],
                                 None, ALU.mult), sem=s_tb)
            p.wait("scalar", s_tb)
            p.op("scalar", lambda sc:
                 sc.activation(wdp[:, :], tscr[:, 0:16], AF.Exp), sem=s_tb)
            p.op("vector", lambda v: v.memset(Sst[:, :], 0.0), sem=s_tb)
            p.op("vector", lambda v: v.memset(Sbf[:, :], 0.0), sem=s_tb)

            p.wait("vector", s_tb)
            p.op("vector", lambda v, hb=hb, h=h:
                 v.scalar_tensor_tensor(pbufU[:, :], kfp[:, hb:hb + S],
                                        tfb[:, h:h + 1], rfp[:, hb:hb + S],
                                        ALU.mult, ALU.mult), sem=s_lg)
            p.wait("tensor", s_lg)
            for ts in range(8):
                p.op("tensor", lambda t, ts=ts:
                     t.matmul(psCl[:, :], ones128b[:, :],
                              pbufU[:, ts * 512:(ts + 1) * 512],
                              start=True, stop=True))
                p.tnop(s_lgm)
                p.wait("vector", s_lgm)
                p.op("vector", lambda v, hb=hb, ts=ts:
                     v.tensor_mul(oacc[:, ts * 512:(ts + 1) * 512],
                                  psCl[:, :], vfp[:, hb + ts * 512: hb + (ts + 1) * 512]),
                     sem=s_lg)
                p.wait("tensor", s_lg)
            for d in range(15):
                w = 15 - d
                nv = 256 * w
                p.wait("vector", s_lg)
                p.op("vector", lambda v, hb=hb, d=d, w=w:
                     v.scalar_tensor_tensor(
                         pbuf[:, 0:256 * w],
                         bass.AP(kfp, hb, [[4 * S, 128], [16, 256], [1, w]]),
                         wdp[:, d:d + 1],
                         bass.AP(rfp, hb + d + 1, [[4 * S, 128], [16, 256], [1, w]]),
                         ALU.mult, ALU.mult), sem=s_lg)
                p.wait("tensor", s_lg)
                bp = min(512 // w, 256)
                nb = 0
                while nb < 256:
                    bpc = min(bp, 256 - nb)
                    N = bpc * w
                    p.op("tensor", lambda t, nb=nb, N=N, w=w:
                         t.matmul(psCl[:, 0:N], ones128b[:, :],
                                  pbuf[:, nb * w: nb * w + N],
                                  start=True, stop=True))
                    p.tnop(s_lgm)
                    p.wait("vector", s_lgm)
                    p.op("vector", lambda v, hb=hb, nb=nb, bpc=bpc, w=w, d=d, N=N:
                         v.tensor_mul(tmpb[:, 0:N], psCl[:, 0:N],
                                      bass.AP(vfp, hb + nb * 16,
                                              [[4 * S, 128], [16, bpc], [1, w]])),
                         sem=s_lg)
                    p.wait("vector", s_lg)
                    p.op("vector", lambda v, hb=hb, nb=nb, bpc=bpc, w=w, d=d, N=N:
                         v.tensor_add(
                             bass.AP(oacc, nb * 16 + d + 1, [[S, 128], [16, bpc], [1, w]]),
                             bass.AP(oacc, nb * 16 + d + 1, [[S, 128], [16, bpc], [1, w]]),
                             tmpb[:, 0:N]), sem=s_lg)
                    p.wait("vector", s_lg)
                    p.wait("tensor", s_lg)
                    nb += bpc

            for hf in range(2):
                base = hf * 2048
                p.wait("vector", s_cm)
                for name, tb in SCB.items():
                    src = rfp if name.startswith("R") else kfp
                    p.op("vector", lambda v, hb=hb, base=base, tb=tb, src=src:
                         v.tensor_tensor(
                             scb[:, tb * 2048:(tb + 1) * 2048],
                             src[:, hb + base: hb + base + 2048],
                             bass.AP(TBL, tb * 128, [[1024, 128], [0, 16], [1, 128]]),
                             ALU.mult), sem=s_ck)
                p.wait("tensor", s_ck)
                for cl in range(16):
                    c = hf * 16 + cl
                    lc = cl * 128
                    tc = hb + c * 128
                    p.wait("tensor", s_ck)
                    p.op("tensor", lambda t, tc=tc:
                         t.transpose(psT2[:, :], vfp[:, tc:tc + 128], identb[:, :]))
                    p.tnop(s_cm)
                    p.wait("vector", s_cm)
                    p.op("vector", lambda v: v.tensor_copy(vtp[:, :], psT2[:, :]),
                         sem=s_ck)
                    p.wait("tensor", s_ck)
                    p.op("tensor", lambda t, lc=lc:
                         t.transpose(psT2[:, :], scb[:, SCB["K128"] * 2048 + lc:
                                                     SCB["K128"] * 2048 + lc + 128],
                                     identb[:, :]))
                    p.tnop(s_cm)
                    p.wait("vector", s_cm)
                    p.op("vector", lambda v: v.tensor_copy(ktp[:, :], psT2[:, :]),
                         sem=s_ck)
                    p.wait("tensor", s_ck)
                    p.op("tensor", lambda t:
                         t.matmul(psA[:, :], zrow[0:1, :], zrow[0:1, :],
                                  start=True, stop=False))
                    for g in (1, 2, 3):
                        for b in range(4 - g):
                            a = b + g
                            p.op("tensor", lambda t, g=g, b=b, a=a, lc=lc:
                                 t.matmul(psA[32 * b:32 * b + 32, 32 * a:32 * a + 32],
                                          scb[:, SCB[f"KB{g}"] * 2048 + lc + 32 * b:
                                              SCB[f"KB{g}"] * 2048 + lc + 32 * b + 32],
                                          scb[:, SCB["R32"] * 2048 + lc + 32 * a:
                                              SCB["R32"] * 2048 + lc + 32 * a + 32],
                                          start=False, stop=False))
                    for a in range(4):
                        tp = (0, 96) if a == 3 else None
                        p.op("tensor", lambda t, a=a, lc=lc, tp=tp:
                             t.matmul(psA[32 * a:32 * a + 16, 32 * a + 16:32 * a + 32],
                                      scb[:, SCB["K16"] * 2048 + lc + 32 * a:
                                          SCB["K16"] * 2048 + lc + 32 * a + 16],
                                      scb[:, SCB["R16"] * 2048 + lc + 32 * a + 16:
                                          SCB["R16"] * 2048 + lc + 32 * a + 32],
                                      start=False, stop=(a == 3),
                                      tile_position=tp))
                    p.tnop(s_cm)
                    p.wait("vector", s_cm)
                    p.op("vector", lambda v: v.tensor_copy(Ab[:, :], psA[:, :]),
                         sem=s_ck)
                    p.wait("tensor", s_ck)
                    p.op("tensor", lambda t, lc=lc:
                         t.matmul(psO[:, :], scb[:, SCB["R128"] * 2048 + lc:
                                                  SCB["R128"] * 2048 + lc + 128],
                                  Sbf[:, :], start=True, stop=False))
                    p.op("tensor", lambda t:
                         t.matmul(psO[:, :], Ab[:, :], vtp[:, :],
                                  start=False, stop=True))
                    p.op("tensor", lambda t:
                         t.matmul(psKV[:, :], ktp[:, :], vtp[:, :],
                                  start=True, stop=True))
                    p.tnop(s_cm)
                    p.wait("vector", s_cm)
                    p.op("vector", lambda v:
                         v.scalar_tensor_tensor(Sst[:, :], Sst[:, :], wdp[:, 15:16],
                                                psKV[:, :], ALU.mult, ALU.add),
                         sem=s_ck)
                    p.wait("vector", s_ck)
                    p.op("vector", lambda v: v.tensor_copy(Sbf[:, :], Sst[:, :]),
                         sem=s_ck)
                    p.op("vector", lambda v: v.tensor_copy(otb[:, :], psO[:, :]),
                         sem=s_ck)
                    p.wait("tensor", s_ck)
                    p.op("tensor", lambda t:
                         t.transpose(psT2[:, :], otb[:, :], identb[:, :]))
                    p.tnop(s_cm)
                    p.wait("vector", s_cm)
                    p.op("vector", lambda v, c=c:
                         v.tensor_add(t1b[:, :], psT2[:, :],
                                      oacc[:, c * 128:(c + 1) * 128]), sem=s_ck)
                    p.wait("vector", s_ck)
                    p.op("vector", lambda v, tc=tc:
                         v.tensor_mul(ogb[:, :], t1b[:, :], gsig[:, tc:tc + 128]),
                         sem=s_ck)
                    p.wait("sync", s_ck)
                    p.dma("sync", ogd[h * 128:(h + 1) * 128, c * 128:(c + 1) * 128],
                          ogb[:, :], s_sp2)
                    p.wait("vector", s_sp2)

        if debug:
            p.wait("sync", s_sp2)
            s_dbg2 = p.sem("dbg2")
            for h in range(4):
                p.dma("sync", dbg_og[h * 128:(h + 1) * 128, :],
                      ogd[h * 128:(h + 1) * 128, :], s_dbg2)
            p.wait("sync", s_dbg2)

        if upto == "wkv":
            wkvctx.close()
            s_end = p.sem("endX")
            p.wait("sync", s_sp2)
            p.dma("sync", out[:, :], rs_out[:, :], s_end)
            p.wait("sync", s_end)
            p.emit()
            return nc
        wkvctx.close()
        wo = _sb(ctx, nc, "woE", [128, 4 * D], BF16)
        wostg = _sb(ctx, nc, "wostgE", [128, D])
        ogs = [_sb(ctx, nc, f"ogsE{k}", [128, 4 * 512], BF16) for k in range(2)]
        ob = [_sb(ctx, nc, f"obE{k}", [128, 512], BF16) for k in range(2)]
        psY = [_ps(ctx, nc, f"psYE{k}", [128, 512]) for k in range(2)]
        s_wo = p.sem("swoE")
        s_og = p.sem("sogE")
        s_yr = p.sem("yrE")
        s_y1 = p.sem("y1E")
        s_y2 = p.sem("y2E")
        s_cc = p.sem("ccE")
        s_st = p.sem("stE")
        p.wait("sync", s_sp2)
        for e in range(4):
            if e >= 1:
                p.wait("sync", s_wo)
            p.dma("sync", wostg[:, :], WoT[e * 128:(e + 1) * 128, :], s_wo)
            p.wait("vector", s_wo)
            p.op("vector", lambda v, e=e:
                 v.tensor_copy(wo[:, e * D:(e + 1) * D], wostg[:, :]), sem=s_wo)
        ky = 0
        for si, (c0, c1) in enumerate(strips(S)):
            sp = si % 2
            if si >= 2:
                p.wait("sync", s_y1, s_y1.v - 1)
            for e in range(4):
                p.dma("sync", ogs[sp][:, e * 512:(e + 1) * 512],
                      ogd[e * 128:(e + 1) * 128, c0:c1], s_og)
            p.wait("tensor", s_og)
            p.wait("tensor", s_wo)
            for dmc in range(16):
                par = ky % 2
                if ky >= 2:
                    p.wait("tensor", s_y2, s_y2.v - 33)
                for e in range(4):
                    p.op("tensor", lambda t, dmc=dmc, e=e, sp=sp, par=par:
                         t.matmul(psY[par][:, :],
                                  wo[:, e * D + dmc * 128: e * D + (dmc + 1) * 128],
                                  ogs[sp][:, e * 512:(e + 1) * 512],
                                  start=(e == 0), stop=(e == 3)))
                if dmc == 15:
                    p.tnop(s_y1)
                p.tnop(s_yr)
                p.wait("vector", s_yr)
                if ky >= 2:
                    p.wait("vector", s_y2, s_y2.v - 17)
                p.op("vector", lambda v, par=par:
                     v.tensor_copy(ob[par][:, :], psY[par][:, :]), sem=s_y2)
                p.wait("sync", s_y2)
                qblk = c0 // NT
                cc0 = c0 - qblk * NT
                p.dma("sync", rs_in[qblk * D + dmc * 128: qblk * D + (dmc + 1) * 128,
                                    cc0:cc0 + 512], ob[par][:, :], s_y2)
                ky += 1
        p.wait("gpsimd", s_y2)
        p.op("gpsimd", lambda g: g.collective_compute(
            "ReduceScatter", ALU.add, replica_groups=[[0, 1, 2, 3], [4, 5, 6, 7]],
            ins=[rs_in.ap().opt()], outs=[rs_out.ap().opt()]), sem=s_cc)
        p.wait("sync", s_cc)
        p.dma("sync", out[:, :], rs_out[:, :], s_st)
        p.wait("sync", s_st)
        p.emit()
    return nc

def stage_xT(x):
    return np.ascontiguousarray(x.reshape(T, D).T)


_SPLINE_CACHE = {}


def spline_grid(n_ctrl, n):
    key = (n_ctrl, n)
    if key not in _SPLINE_CACHE:
        t = np.linspace(0.0, n_ctrl - 1.0, n, dtype=np.float64)
        i = np.clip(np.floor(t).astype(np.int64), 0, n_ctrl - 2)
        f = (t - i).astype(np.float32).astype(np.float64)
        idx = np.stack([np.clip(i + k, 0, n_ctrl - 1) for k in (-1, 0, 1, 2)],
                       axis=0).astype(np.int64)
        c = np.stack([
            0.5 * (-f + 2 * f ** 2 - f ** 3),
            0.5 * (2 - 5 * f ** 2 + 3 * f ** 3),
            0.5 * (f + 4 * f ** 2 - 3 * f ** 3),
            0.5 * (-(f ** 2) + f ** 3)], axis=0).astype(np.float32)
        _SPLINE_CACHE[key] = (idx, c)
    return _SPLINE_CACHE[key]


def stage_spline_gateup(cp, core):
    idx, c = spline_grid(cp.shape[0], INTER0 * D)
    i0 = core * ISL0
    J = ((i0 + np.arange(ISL0))[None, :] * D + np.arange(D)[:, None])
    ps = np.ascontiguousarray(np.concatenate([cp[idx[k][J]] for k in range(4)], axis=1))
    cs = np.ascontiguousarray(np.concatenate([c[k][J] for k in range(4)], axis=1))
    return ps, cs


def stage_spline_down(cp, core):
    idx, c = spline_grid(cp.shape[0], D * INTER0)
    i0 = core * ISL0
    J = (np.arange(D)[None, :] * INTER0 + i0 + np.arange(ISL0)[:, None])
    ps = np.ascontiguousarray(np.concatenate([cp[idx[k][J]] for k in range(4)], axis=1))
    cs = np.ascontiguousarray(np.concatenate([c[k][J] for k in range(4)], axis=1))
    return ps, cs


def stage_moe(inputs, core):
    gw = np.asarray(inputs["L1_ffn_gate_w"])
    w1 = np.asarray(inputs["L1_ffn_w1"])
    w2 = np.asarray(inputs["L1_ffn_w2"])
    sw1 = np.asarray(inputs["L1_ffn_shared_w1"])
    sw2 = np.asarray(inputs["L1_ffn_shared_w2"])
    e0, e1 = 2 * core, 2 * core + 1
    perm = [e0, e1] + [e for e in range(NE) if e not in (e0, e1)]
    m = {
        "gwT": np.ascontiguousarray(gw[perm].T),
        "w1T0": np.ascontiguousarray(w1[e0].T),
        "w1T1": np.ascontiguousarray(w1[e1].T),
        "w2T0": np.ascontiguousarray(w2[e0].T),
        "w2T1": np.ascontiguousarray(w2[e1].T),
        "sw1gT": np.ascontiguousarray(sw1[352 * core:352 * (core + 1)].T),
        "sw1uT": np.ascontiguousarray(sw1[ISH + 352 * core:ISH + 352 * (core + 1)].T),
        "sw2T": np.ascontiguousarray(sw2[:, 352 * core:352 * (core + 1)].T),
        "identF": np.eye(128, dtype=np.float32),
    }
    return m


def stage_exps():
    c = np.arange(128, dtype=np.float64)
    blocks = [
        c,
        127 - c,
        c % 32,
        31 - (c % 32),
        63 - (c % 32),
        95 - (c % 32),
        c % 16,
        15 - (c % 16),
    ]
    e = np.concatenate(blocks + [np.arange(15, dtype=np.float64), [128.0]])
    return np.ascontiguousarray(
        np.broadcast_to(e[None, :], (128, 1040)).astype(np.float32))


def stage_wkv(inputs, core):
    hg = core % 4
    rs = slice(512 * hg, 512 * (hg + 1))
    m = {
        "E_WrT": np.ascontiguousarray(np.asarray(inputs["L1_attn_Wr"])[rs].T),
        "E_WkT": np.ascontiguousarray(np.asarray(inputs["L1_attn_Wk"])[rs].T),
        "E_WvT": np.ascontiguousarray(np.asarray(inputs["L1_attn_Wv"])[rs].T),
        "E_WgT": np.ascontiguousarray(np.asarray(inputs["L1_attn_Wg"])[rs].T),
        "E_WoT": np.ascontiguousarray(np.asarray(inputs["L1_attn_Wo"])[:, rs].T),
        "tdh": np.ascontiguousarray(np.asarray(inputs["L1_attn_time_decay"])[4 * hg:4 * hg + 4].T),
        "tfh": np.ascontiguousarray(np.asarray(inputs["L1_attn_time_first"])[4 * hg:4 * hg + 4].T),
        "tmk": np.asarray(inputs["L1_attn_tm_k"]).reshape(D, 1).astype(np.float32),
        "tmv": np.asarray(inputs["L1_attn_tm_v"]).reshape(D, 1).astype(np.float32),
        "tmr": np.asarray(inputs["L1_attn_tm_r"]).reshape(D, 1).astype(np.float32),
        "tmg": np.asarray(inputs["L1_attn_tm_g"]).reshape(D, 1).astype(np.float32),
        "exps": stage_exps(),
        "identE": np.eye(128, dtype=np.float32).astype(BF16NP),
    }
    return m


def build_fused():
    nc = bass.Bass("TRN2", target_bir_lowering=False)
    xT = nc.dram_tensor("xT", [D, NT], F32, kind="ExternalInput")
    xn_sh = nc.dram_tensor("xn_sh", [D, NT], BF16)
    xnAG4 = nc.dram_tensor("xnAG4", [4 * D, NT], BF16)
    xnAG8 = nc.dram_tensor("xnAG8", [8 * D, NT], BF16, addr_space="Shared")
    y0s = nc.dram_tensor("y0s", [D, NT], BF16)
    f0s = nc.dram_tensor("f0s", [D, NT], BF16)
    x1i = nc.dram_tensor("x1i", [D, NT], F32)
    xn1_sh = nc.dram_tensor("xn1_sh", [D, NT], BF16)
    xn1AG4 = nc.dram_tensor("xn1AG4", [4 * D, NT], BF16)
    xn1AG8 = nc.dram_tensor("xn1AG8", [8 * D, NT], BF16, addr_space="Shared")
    y1s = nc.dram_tensor("y1s", [D, NT], BF16)
    f1s = nc.dram_tensor("f1s", [D, NT], BF16)

    mctx = ExitStack()
    p0 = Prog(nc, mctx)
    rsB = p0.sem("rsB")
    rsE = p0.sem("rsE")
    p0.emit()

    def allgather(src, dst4, dst8):
        c2 = ExitStack()
        with c2:
            p2 = Prog(nc, c2)
            sg = p2.sem("ag")
            p2.op("gpsimd", lambda g: g.collective_compute(
                "AllGather", ALU.bypass, replica_groups=[[0, 1, 2, 3], [4, 5, 6, 7]],
                ins=[src.ap().opt()], outs=[dst4.ap().opt()]), sem=sg)
            p2.op("gpsimd", lambda g: g.collective_compute(
                "AllGather", ALU.bypass, replica_groups=[list(range(8))],
                ins=[src.ap().opt()], outs=[dst8.ap().opt()]), sem=sg)
            p2.wait("gpsimd", sg)
            p2.emit()

    build_launch_A(nc=nc, io={"xT": xT, "xnT": xn_sh})
    allgather(xn_sh, xnAG4, xnAG8)
    build_launch_B(nc=nc, io={"xnB": (xnAG4, "rank", 0), "y0sum": y0s})
    build_launch_C(nc=nc, io={"xnF": (xnAG8, "rank", 0), "ffn0sum": f0s})
    build_launch_D(nc=nc, io={"xT": xT, "y0": y0s, "f0": f0s,
                              "x1T": x1i, "xn1T": xn1_sh})
    allgather(xn1_sh, xn1AG4, xn1AG8)
    build_launch_E(nc=nc, io={"xn1B": (xn1AG4, "rank", 0), "y1sum": y1s})
    build_launch_F(nc=nc, io={"xnF": (xn1AG8, "rank", 0), "ffn1sum": f1s})
    build_launch_G(nc=nc, io={"x1T": x1i, "y1": y1s, "f1": f1s})
    mctx.close()
    return nc


def kernel(**inputs):
    cores = list(range(8))
    x = np.asarray(inputs["L0_attn_x"], dtype=np.float32)
    xT = stage_xT(x)
    masks = stage_masks()
    lnw = np.asarray(inputs["L1_attn_ln_w"]).reshape(D, 1).astype(np.float32)
    lnb = np.asarray(inputs["L1_attn_ln_b"]).reshape(D, 1).astype(np.float32)
    in_maps = []
    for c in cores:
        m = {"xT": np.ascontiguousarray(xT[:, c * NT:(c + 1) * NT]),
             "masks": masks, "lnw": lnw, "lnb": lnb}
        WqT, WkT, WvT, WoT = stage_attn0_weights(inputs, c)
        m.update({"WqT": WqT, "WkT": WkT, "WvT": WvT, "WoT": WoT})
        m["g_p"], m["g_c"] = stage_spline_gateup(np.asarray(inputs["L0_ffn_gate_cp"]), c)
        m["u_p"], m["u_c"] = stage_spline_gateup(np.asarray(inputs["L0_ffn_up_cp"]), c)
        m["d_p"], m["d_c"] = stage_spline_down(np.asarray(inputs["L0_ffn_down_cp"]), c)
        m.update(stage_wkv(inputs, c))
        m.update(stage_moe(inputs, c))
        in_maps.append(m)
    res = run_bass_kernel_spmd(_nc("FUSED", build_fused), in_maps, core_ids=cores)
    x2 = np.concatenate([res.results[c]["x2T"].T for c in cores], axis=0)
    return np.ascontiguousarray(x2.reshape(B, S, D))


_NC_CACHE = {}


def _nc(name, builder):
    if name not in _NC_CACHE:
        _NC_CACHE[name] = builder()
    return _NC_CACHE[name]


def kernel_multi(**inputs):
    cores = list(range(8))
    x = np.asarray(inputs["L0_attn_x"], dtype=np.float32)
    xT = stage_xT(x)
    xT_shards = [np.ascontiguousarray(xT[:, c * NT:(c + 1) * NT]) for c in cores]

    resA = run_bass_kernel_spmd(_nc("A", build_launch_A),
                                [{"xT": xT_shards[c]} for c in cores], core_ids=cores)
    xn_shards = [resA.results[c]["xnT"] for c in cores]
    xnF = np.ascontiguousarray(np.concatenate(xn_shards, axis=1))
    xnB = [np.ascontiguousarray(xnF[:, b * S:(b + 1) * S]) for b in range(B)]

    masks = stage_masks()
    mapsB = []
    for c in cores:
        WqT, WkT, WvT, WoT = stage_attn0_weights(inputs, c)
        mapsB.append({"xnB": xnB[c // 4], "WqT": WqT, "WkT": WkT, "WvT": WvT,
                      "WoT": WoT, "masks": masks})
    resB = run_bass_kernel_spmd(_nc("B", build_launch_B), mapsB, core_ids=cores)
    y0 = [resB.results[c]["y0sum"] for c in cores]

    mapsC = []
    for c in cores:
        m = {"xnF": xnF}
        m["g_p"], m["g_c"] = stage_spline_gateup(np.asarray(inputs["L0_ffn_gate_cp"]), c)
        m["u_p"], m["u_c"] = stage_spline_gateup(np.asarray(inputs["L0_ffn_up_cp"]), c)
        m["d_p"], m["d_c"] = stage_spline_down(np.asarray(inputs["L0_ffn_down_cp"]), c)
        mapsC.append(m)
    resC = run_bass_kernel_spmd(_nc("C", build_launch_C), mapsC, core_ids=cores)
    f0 = [resC.results[c]["ffn0sum"] for c in cores]

    mapsD = [{"xT": xT_shards[c], "y0": y0[c], "f0": f0[c]} for c in cores]
    resD = run_bass_kernel_spmd(_nc("D", build_launch_D), mapsD, core_ids=cores)
    x1_shards = [resD.results[c]["x1T"] for c in cores]
    xn1_shards = [resD.results[c]["xn1T"] for c in cores]
    xn1F = np.ascontiguousarray(np.concatenate(xn1_shards, axis=1))
    xn1B = [np.ascontiguousarray(xn1F[:, b * S:(b + 1) * S]) for b in range(B)]

    mapsE = []
    for c in cores:
        m = stage_wkv(inputs, c)
        m["xn1B"] = xn1B[c // 4]
        mapsE.append(m)
    resE = run_bass_kernel_spmd(_nc("E", build_launch_E), mapsE, core_ids=cores)
    y1 = [resE.results[c]["y1sum"] for c in cores]

    mapsF = []
    for c in cores:
        m = stage_moe(inputs, c)
        m["xnF"] = xn1F
        mapsF.append(m)
    resF = run_bass_kernel_spmd(_nc("F", build_launch_F), mapsF, core_ids=cores)
    f1 = [resF.results[c]["ffn1sum"] for c in cores]

    lnw = np.asarray(inputs["L1_attn_ln_w"]).reshape(D, 1).astype(np.float32)
    lnb = np.asarray(inputs["L1_attn_ln_b"]).reshape(D, 1).astype(np.float32)
    mapsG = [{"x1T": x1_shards[c], "y1": y1[c], "f1": f1[c], "lnw": lnw, "lnb": lnb}
             for c in cores]
    resG = run_bass_kernel_spmd(_nc("G", build_launch_G), mapsG, core_ids=cores)
    x2 = np.concatenate([resG.results[c]["x2T"].T for c in cores], axis=0)
    return np.ascontiguousarray(x2.reshape(B, S, D))
